# version 09
# speedup vs baseline: 1.4225x; 1.0089x over previous
import numpy as np
import ml_dtypes
import concourse.bass as bass
import concourse.mybir as mybir
from concourse.bass_utils import run_bass_kernel_spmd

F32 = mybir.dt.float32
BF16 = mybir.dt.bfloat16
AF = mybir.ActivationFunctionType
ALU = mybir.AluOpType
BF = ml_dtypes.bfloat16

D, S, NCH, TB, NTB = 1024, 2048, 8, 512, 4
NH, DH, NPAIR, ROT, NB = 16, 64, 8, 32, 2
N_STAGES = 4
DEBUG = 0
FLUSH_OFF = 0
RACE_HOOK = None


class Prog:
    ENGINES = ("sync", "scalar", "vector", "tensor", "gpsimd")
    TRACE = False

    def __init__(self):
        self.ops = {e: [] for e in self.ENGINES}
        self.needed = set()
        self.kind = {}
        self.order = []
        self.src = {}

    def emit(self, eng, fn, deps=(), kind="compute"):
        h = (eng, len(self.ops[eng]))
        self.ops[eng].append((fn, tuple(d for d in deps if d is not None), kind))
        self.kind[h] = kind
        self.order.append(h)
        if Prog.TRACE:
            import sys
            f = sys._getframe(1)
            while f and f.f_code.co_name in ("emit", "dma"):
                f = f.f_back
            self.src[h] = (f.f_code.co_name, f.f_lineno) if f else ("?", 0)
        for d in deps:
            if d is not None:
                self.needed.add(d)
        return h

    flush = None

    def dma(self, eng, out, in_, deps=(), flush=True):
        h = self.emit(eng, lambda e, o=out, i=in_: e.dma_start(out=o, in_=i),
                      deps, kind="dma")
        self.needed.add(h)
        if flush and self.flush is not None:
            fd, fs = self.flush[eng]
            h = self.emit(eng, lambda e, o=fd, i=fs:
                          e.dma_start(out=o, in_=i), (h,), kind="dma")
            self.needed.add(h)
        return h

    def barrier_handles(self):
        return [(e, len(self.ops[e]) - 1) for e in self.ENGINES if self.ops[e]]

    def runner(self, sems):
        sig_val = {}
        for e in self.ENGINES:
            cum = {"dma": 0, "compute": 0}
            for i, (fn, deps, kind) in enumerate(self.ops[e]):
                if (e, i) in self.needed:
                    cum[kind] += 16 if kind == "dma" else 1
                    sig_val[(e, i)] = cum[kind]

        def run_engine(e, eng_obj):
            waited = {}
            for i, (fn, deps, kind) in enumerate(self.ops[e]):
                for d in deps:
                    de, di = d
                    dk = self.kind[d]
                    if de == e:
                        assert di < i, f"forward same-engine dep {d} -> {(e, i)}"
                    v = sig_val[d]
                    if waited.get((de, dk), -1) >= v:
                        continue
                    waited[(de, dk)] = v
                    eng_obj.wait_ge(sems[(de, dk)], v)
                ins = fn(eng_obj)
                if (e, i) in self.needed:
                    ins.then_inc(sems[(e, kind)], 16 if kind == "dma" else 1)
        return run_engine


def host_constants():
    c = {}
    p = np.arange(128)[:, None]
    w = np.arange(896)[None, :]
    c["diagm"] = (w >= p + 384).astype(BF)
    wdx = np.arange(2080)[None, :]
    c["stair2"] = ((wdx - 16) < 16 * p).astype(BF)
    c["iota128"] = np.arange(128, dtype=np.float32).reshape(128, 1)
    gam = 1.0 - 2.0 ** (-5.0 - np.arange(NH, dtype=np.float64))
    t = np.arange(S, dtype=np.float64)
    c["gq_rows"] = np.exp(np.log(gam)[:, None] * t[None, :]).astype(np.float32)
    kk = np.arange(16)[None, :, None] * 128 + np.arange(128)[:, None, None]
    c["gk_cols"] = np.exp(-np.log(gam)[None, None, :] * kk).reshape(128, 256).astype(np.float32)
    half = np.arange(0, ROT, 2, dtype=np.float64)
    inv_freq = 1.0 / (10000.0 ** (half / ROT))
    fr = t[:, None] * inv_freq[None, :]
    sv = (half + 0.4 * ROT) / (1.4 * ROT)
    power = (t - S // 2) / 512.0
    sch = sv[np.arange(ROT) % 16][None, :] ** power[:, None]
    cos_h, sin_h = np.cos(fr), np.sin(fr)

    def head_tables(s, g):
        gt = g[None, :]
        A = np.zeros((64, S))
        B = np.zeros((64, S))
        A[0:16] = (cos_h * s[:, 0::2]).T * gt
        A[16:32] = (cos_h * s[:, 1::2]).T * gt
        A[32:64] = gt
        B[0:16] = (-sin_h * s[:, 0::2]).T * gt
        B[16:32] = (sin_h * s[:, 1::2]).T * gt
        return A, B
    gam = 1.0 - 2.0 ** (-5.0 - np.arange(NH, dtype=np.float64))
    for pr in range(NPAIR):
        for nm, sc_, gpow in (("q", sch, t), ("k", 1.0 / sch, -t)):
            full_A = np.zeros((128, S))
            full_B = np.zeros((128, S))
            for j, hh in enumerate((2 * pr, 2 * pr + 1)):
                g = np.exp(np.log(gam[hh]) * gpow)
                A, B = head_tables(sc_, g)
                full_A[64 * j:64 * j + 64] = A
                full_B[64 * j:64 * j + 64] = B
            c[f"rotA{nm}{pr}"] = full_A.astype(np.float32)
            c[f"rotB{nm}{pr}"] = full_B.astype(np.float32)
    e = np.arange(16)
    c["moe_lt"] = (e[None, :] < e[:, None]).astype(BF).reshape(1, 256)
    g = np.arange(4)
    c["grp_lt"] = (g[None, :] < g[:, None]).astype(np.float32).reshape(1, 16)
    c["flushz"] = np.zeros((1, 4), np.float32)
    return c


def prep_weights(p):
    C = np.ascontiguousarray
    w = {}
    w["l0_wq"] = C(p["L0_attn_Wq"]); w["l0_wk"] = C(p["L0_attn_Wk"])
    w["l0_wv"] = C(p["L0_attn_Wv"]); w["l0_wo"] = C(p["L0_attn_Wo"])
    w["l0_span"] = C(p["L0_attn_span_val"].reshape(1, NH))
    w["rt_col"] = C(p["L0_ffn_routing_token"].reshape(NCH, 128).T)
    w["gam_l"] = C(p["L0_ffn_gamma_light"].reshape(NCH, 128).T)
    w["gam_h"] = C(p["L0_ffn_gamma_heavy"].reshape(NCH, 128).T)
    w["w1l"] = C(p["L0_ffn_w1_light"]); w["b1l"] = C(p["L0_ffn_b1_light"].reshape(4, 128).T)
    w["w2l"] = C(p["L0_ffn_w2_light"]); w["b2l"] = C(p["L0_ffn_b2_light"].reshape(8, 128).T)
    w["w1h"] = C(p["L0_ffn_w1_heavy"]); w["b1h"] = C(p["L0_ffn_b1_heavy"].reshape(32, 128).T)
    w["w2h"] = C(p["L0_ffn_w2_heavy"]); w["b2h_row"] = C(p["L0_ffn_b2_heavy"].reshape(1, 1024))
    perm = np.concatenate([np.arange(0, 32, 2), np.arange(1, 32, 2), np.arange(32, 64)])
    wqkv = p["L1_attn_wqkv_w"].reshape(D, NH, 3 * DH)
    w["l1_wq"] = C(wqkv[:, :, 0:DH][:, :, perm].reshape(D, D))
    w["l1_wk"] = C(wqkv[:, :, DH:2 * DH][:, :, perm].reshape(D, D))
    w["l1_wv"] = C(wqkv[:, :, 2 * DH:].reshape(D, D))
    sperm = np.concatenate([np.arange(1, 32, 2), np.arange(0, 32, 2)])

    def pack_swap(ws):
        out = np.zeros((D, NPAIR, 128), dtype=np.float32)
        for pr in range(NPAIR):
            out[:, pr, 0:32] = ws[:, 2 * pr]
            out[:, pr, 64:96] = ws[:, 2 * pr + 1]
        return C(out.reshape(D, NPAIR * 128))
    w["l1_wqs"] = pack_swap(wqkv[:, :, 0:DH][:, :, sperm])
    w["l1_wks"] = pack_swap(wqkv[:, :, DH:2 * DH][:, :, sperm])
    w["l1_ln1_w"] = C(p["L1_attn_ln1_w"].reshape(NCH, 128).T)
    w["l1_ln1_b"] = C(p["L1_attn_ln1_b"].reshape(NCH, 128).T)
    w["l1_wg"] = C(p["L1_attn_wg_w"]); w["l1_wo"] = C(p["L1_attn_wo_w"])
    w["l1_gn_w"] = C(p["L1_attn_gn_w"].reshape(1, NH))
    w["l1_gn_b"] = C(p["L1_attn_gn_b"].reshape(1, NH))
    w["l1_ln2_w"] = C(p["L1_attn_ln2_w"].reshape(NCH, 128).T)
    w["l1_ln2_b"] = C(p["L1_attn_ln2_b"].reshape(NCH, 128).T)
    w["l1_w1"] = C(p["L1_attn_w1_w"]); w["l1_b1"] = C(p["L1_attn_w1_b"].reshape(8, 128).T)
    w["l1_w2"] = C(p["L1_attn_w2_w"]); w["l1_b2"] = C(p["L1_attn_w2_b"].reshape(8, 128).T)
    w["router_wT"] = C(p["L1_ffn_router_w"].T)
    w["corr_bias"] = C(p["L1_ffn_correction_bias"].reshape(1, 16))
    w["moe_gu"] = C(np.concatenate([p["L1_ffn_gate_w"], p["L1_ffn_up_w"]], axis=2))
    w["moe_down"] = C(p["L1_ffn_down_w"])
    w["sh_guT"] = C(np.concatenate([p["L1_ffn_shared_gate_w"].T,
                                    p["L1_ffn_shared_up_w"].T], axis=1))
    w["sh_downT"] = C(p["L1_ffn_shared_down_w"].T)

    def swz(W):
        K, M = W.shape
        mw = 128 if M % 128 == 0 else M
        Cn, Mc = K // 128, M // mw
        return C(W.reshape(Cn, 128, Mc, mw).transpose(2, 1, 0, 3)
                 .reshape(Mc, 128, Cn * mw))
    for nm in ("l0_wq", "l0_wk", "l0_wo", "w1l", "w2l", "w1h", "w2h",
               "l1_wq", "l1_wk", "l1_wqs", "l1_wks", "l1_wg", "l1_wo",
               "l1_w1", "l1_w2", "sh_downT", "sh_guT", "router_wT"):
        w[nm + "_s"] = swz(w.pop(nm))
    w["moe_gu_s"] = C(np.stack([swz(w["moe_gu"][e]) for e in range(16)]))
    w["moe_down_s"] = C(np.stack([swz(w["moe_down"][e]) for e in range(16)]))
    del w["moe_gu"], w["moe_down"]
    return w


WSHAPES = {
    "l0_wv": [D, D],
    "l0_span": [1, NH], "rt_col": [128, NCH],
    "gam_l": [128, NCH], "gam_h": [128, NCH],
    "b1l": [128, 4], "b2l": [128, NCH],
    "b1h": [128, 32], "b2h_row": [1, D],
    "l1_wv": [D, D],
    "l1_ln1_w": [128, NCH], "l1_ln1_b": [128, NCH],
    "l1_gn_w": [1, NH], "l1_gn_b": [1, NH],
    "l1_ln2_w": [128, NCH], "l1_ln2_b": [128, NCH],
    "l1_b1": [128, NCH], "l1_b2": [128, NCH],
    "corr_bias": [1, 16],
    "l0_wq_s": [8, 128, D], "l0_wk_s": [8, 128, D], "l0_wo_s": [8, 128, D],
    "w1l_s": [4, 128, D], "w2l_s": [8, 128, 512],
    "w1h_s": [32, 128, D], "w2h_s": [8, 128, 4096],
    "l1_wq_s": [8, 128, D], "l1_wk_s": [8, 128, D],
    "l1_wqs_s": [8, 128, D], "l1_wks_s": [8, 128, D],
    "l1_wg_s": [8, 128, D], "l1_wo_s": [8, 128, D],
    "l1_w1_s": [8, 128, D], "l1_w2_s": [8, 128, D],
    "sh_downT_s": [8, 128, 2048], "sh_guT_s": [32, 128, D],
    "router_wT_s": [1, 128, 128],
    "moe_gu_s": [16, 4, 128, D], "moe_down_s": [16, 8, 128, 256],
}


class K:
    pass


def build(n_stages=N_STAGES, n_batches=NB):
    nc = bass.Bass("TRN2", target_bir_lowering=False)
    cons = host_constants()
    k = K()
    k.nc = nc
    P = k.P = Prog()

    k.xin = nc.dram_tensor("xin", [NB, NCH, 128, S], F32, kind="ExternalInput").ap()
    k.xout = nc.dram_tensor("xout", [NB, NCH, 128, S], F32, kind="ExternalOutput").ap()
    if DEBUG:
        k.dbg_h = nc.dram_tensor("dbg_h", [NB, 128, NCH * S], BF16, kind="ExternalOutput").ap()
        k.dbg_a = nc.dram_tensor("dbg_a", [NB, 128, NCH * S], BF16, kind="ExternalOutput").ap()
        k.dbg_v = nc.dram_tensor("dbg_v", [NB, 128, NCH * S], BF16, kind="ExternalOutput").ap()
        k.dbg_pv = nc.dram_tensor("dbg_pv", [128, 512], F32, kind="ExternalOutput").ap()
        k.dbg_p = nc.dram_tensor("dbg_p", [128, 2048], BF16, kind="ExternalOutput").ap()
        k.dbg_sta = nc.dram_tensor("dbg_sta", [128, 16 * 66], BF16, kind="ExternalOutput").ap()
        k.dbg_cols = nc.dram_tensor("dbg_cols", [128, 48], F32, kind="ExternalOutput").ap()
        k.dbg_cols2 = nc.dram_tensor("dbg_cols2", [128, 48], F32, kind="ExternalOutput").ap()
        k.dbg_smf = nc.dram_tensor("dbg_smf", [128, 48], F32, kind="ExternalOutput").ap()
        k.dbg_done = []
    k.dc = {}
    for nm, v in cons.items():
        dt = BF16 if v.dtype == BF else F32
        k.dc[nm] = nc.dram_tensor("c_" + nm, list(v.shape), dt, kind="ExternalInput").ap()
    k.dw = {nm: nc.dram_tensor("w_" + nm, shp, F32, kind="ExternalInput").ap()
            for nm, shp in WSHAPES.items()}

    from contextlib import ExitStack
    k.ctx = ExitStack()
    sb = lambda nm, shape, dt: k.ctx.enter_context(nc.sbuf_tensor(nm, shape, dt))[:]
    pst = lambda nm, shape: k.ctx.enter_context(nc.psum_tensor(nm, shape, F32))[:]

    k.X = sb("X", [128, NCH * S], F32)
    k.Y = sb("Y", [128, NCH * S], F32)
    k.Yb = k.Y.bitcast(BF16)
    k.Hb = sb("Hb", [128, NCH * S], BF16)
    k.SC = sb("SCb", [128, 6144], BF16)
    k.SCf = k.SC.bitcast(F32)
    k.WST = sb("WST", [128, 2048], F32)
    k.WB = sb("WB", [128, 2048], BF16)
    k.ROW = sb("ROWX", [1, 2048], F32)
    k.DIAGM = sb("DIAGM", [128, 896], BF16)
    k.STA = sb("STA", [128, 16 * 66], BF16)
    k.SMF = sb("SMF", [128, 768], F32)
    k.SMB = sb("SMB", [128, 1024], BF16)
    k.ROUT = sb("ROUT", [128, 96], F32)
    k.BIASC = sb("BIASC", [128, 64], F32)
    k.STAIR2 = sb("STAIR2", [128, 2080], BF16)
    k.COLS = sb("COLSF", [128, 48], F32)
    k.COLB = sb("COLSB", [128, 16], BF16)
    k.flushd = nc.dram_tensor("flushd", [8, 64], F32, kind="ExternalOutput").ap()
    P.flush = None if FLUSH_OFF else {
        e: (k.flushd[qi:qi + 1, 0:4], k.dc["flushz"][:])
        for qi, e in enumerate(Prog.ENGINES)}
    k.PB = [pst(f"PB{i}", [128, 512]) for i in range(8)]
    k.rem = nc.sbuf_bytes_remaining
    assert k.rem >= 0, f"SBUF over budget by {-k.rem}"
    k.pb_last = [None] * 8
    k.wst_last = [None, None]
    k.wb_last = [None, None]
    k.last_tensor = None

    k.C_ONES, k.C_IOTA, k.C_ONEP = 0, 1, 2
    k.C_MASK16, k.C_MASKP16, k.C_TMP = 3, 19, 35

    k.cc_cache = {}
    k.cc_next = [36]

    def constcol(val):
        if val in k.cc_cache:
            return k.cc_cache[val]
        idx = k.cc_next[0]
        k.cc_next[0] += 1
        assert idx < 48
        ap = k.COLS[:, idx:idx + 1]
        h = P.emit("vector", lambda v, a=ap, x=val: v.memset(a, x))
        k.cc_cache[val] = (ap, h)
        return k.cc_cache[val]
    k.constcol = constcol

    k.h_consts = [P.dma("scalar", k.DIAGM, k.dc["diagm"][:]),
                  P.dma("scalar", k.COLS[:, 1:2], k.dc["iota128"][:]),
                  P.dma("scalar", k.STAIR2, k.dc["stair2"][:])]
    k.h_ones = P.emit("vector", lambda v: v.memset(k.COLS[:, 0:1], 1.0))
    k.h_onesb = P.emit("vector", lambda v: v.memset(k.COLB[:, 0:1], 1.0))
    k.h_onep = P.emit("vector", lambda v: v.tensor_scalar_add(
        k.COLS[:, 2:3], k.COLS[:, 0:1], 1e-8), deps=[k.h_ones])
    k.h_onesrow64 = P.emit("vector", lambda v: v.memset(k.SMF[64:65, 512:576], 1.0))
    k.ONESROW = k.SMF[0:1, 512:640]
    k.h_onesrow0 = P.emit("vector", lambda v: v.memset(k.ONESROW, 1.0))

    k.prev_stores = []
    for b in range(n_batches):
        build_batch(k, b, n_stages)
        bar = P.barrier_handles()
        k.prev_stores = [P.dma("gpsimd", k.xout[b, c], k.X[:, c * S:(c + 1) * S],
                               deps=bar, flush=False) for c in range(NCH)]
    P.emit("vector", lambda v: v.nop(), deps=k.prev_stores)

    if RACE_HOOK is not None:
        RACE_HOOK(k)

    sems = {}
    with k.ctx:
        sem_ctx = ExitStack()
        for e in Prog.ENGINES:
            sems[(e, "compute")] = sem_ctx.enter_context(nc.semaphore("semc_" + e))
            sems[(e, "dma")] = sem_ctx.enter_context(nc.semaphore("semd_" + e))
        with sem_ctx:
            run = P.runner(sems)
            with nc.Block() as block:
                @block.sync
                def _(sync):
                    run("sync", sync)

                @block.scalar
                def _(scalar):
                    run("scalar", scalar)

                @block.vector
                def _(vector):
                    run("vector", vector)

                @block.tensor
                def _(tensor):
                    run("tensor", tensor)

                @block.gpsimd
                def _(gpsimd):
                    run("gpsimd", gpsimd)
    return nc


def dbg_fence(k, handles):
    for e in Prog.ENGINES:
        k.P.emit(e, lambda eng: eng.nop(), deps=handles)


def xch(k, c, tb=None):
    a = k.X[:, c * S:(c + 1) * S]
    return a if tb is None else a[:, tb * TB:(tb + 1) * TB]


def hch(k, c, tb=None):
    a = k.Hb[:, c * S:(c + 1) * S]
    return a if tb is None else a[:, tb * TB:(tb + 1) * TB]


def ln_pass(k, xdeps, dst_fn, bct, double=False, accum=False, src_fn=None,
            tb_hook=None):
    P = k.P
    src_fn = src_fn or (lambda c, tb: (xch(k, c, tb), [xdeps[c]]))
    done = {}
    XBC, XSQ = k.SMB[:, 0:512], k.SMB[:, 512:1024]
    Tt = bct[:, 1024:1536]
    prev_cast = prev_sq = None

    def emit_applies(tb, hbm, hbrs, pb_m, pb_v):
        lastm = lastr = None
        for c in range(NCH):
            xc, xd = src_fn(c, tb)
            ht = P.emit("vector", lambda v, xc=xc, m=pb_m: v.tensor_tensor(
                Tt, xc, k.PB[m], ALU.subtract), deps=[hbm] + xd)
            lastm = ht
            dst, ed = dst_fn(c, tb)
            if not accum:
                ha = P.emit("vector", lambda v, dst=dst, r=pb_v: v.tensor_tensor(
                    dst, Tt, k.PB[r], ALU.mult), deps=[ht, hbrs] + list(ed))
            else:
                ht2 = P.emit("vector", lambda v, r=pb_v: v.tensor_tensor(
                    Tt, Tt, k.PB[r], ALU.mult), deps=[ht, hbrs])
                ha = P.emit("vector", lambda v, dst=dst: v.tensor_add(dst, dst, Tt),
                            deps=[ht2] + list(ed))
                lastr = ht2
            done[(c, tb)] = ha
            lastr = lastr if accum else ha
        k.pb_last[pb_m] = lastm
        k.pb_last[pb_v] = lastr

    staged = []
    prev_hrs = None
    for tb in range(NTB):
        pb_m, pb_v = (6, 7) if tb % 2 == 0 else (4, 5)
        ROWB = k.ROW
        h1 = h2 = None
        for c in range(NCH):
            xc, xd = src_fn(c, tb)
            prev_cast = P.emit("scalar", lambda s, xc=xc: s.activation(
                out=XBC, in_=xc, func=AF.Identity), deps=xd + [h1])
            prev_sq = P.emit("scalar", lambda s, xc=xc: s.activation(out=XSQ, in_=xc,
                                                                     func=AF.Square),
                             deps=xd + [h2])
            h1 = P.emit("tensor", lambda t, m=pb_m, st=(c == 0), sp=(c == NCH - 1):
                        t.matmul(k.PB[m][0:1, :], k.COLB[:, 0:1], XBC, start=st, stop=sp),
                        deps=[prev_cast, k.h_onesb,
                              k.pb_last[pb_m] if c == 0 else None])
            h2 = P.emit("tensor", lambda t, r=pb_v, st=(c == 0), sp=(c == NCH - 1):
                        t.matmul(k.PB[r][0:1, :], k.COLB[:, 0:1], XSQ, start=st, stop=sp),
                        deps=[prev_sq, k.h_onesb,
                              k.pb_last[pb_v] if c == 0 else None])
        rm = ROWB[0:1, 0:512]
        rrs = ROWB[0:1, 512:1024]
        rex2 = k.ROW[0:1, 1024:1536]
        rtmp = k.ROW[0:1, 1536:2048]
        hm = P.emit("vector", lambda v, m=pb_m: v.tensor_scalar_mul(
            rm, k.PB[m][0:1, :], 1.0 / D), deps=[h1])
        hv = P.emit("vector", lambda v, r=pb_v: v.tensor_scalar_mul(
            rex2, k.PB[r][0:1, :], 1.0 / D), deps=[h2, prev_hrs])
        hmsq = P.emit("vector", lambda v: v.tensor_mul(rtmp, rm, rm),
                      deps=[hm, prev_hrs])
        hvv = P.emit("vector", lambda v: v.tensor_sub(rex2, rex2, rtmp),
                     deps=[hv, hmsq])
        epsc, epsh = k.constcol(1e-5)
        hsq = P.emit("scalar", lambda s, e=epsc: s.activation(
            out=rtmp, in_=rex2, func=AF.Sqrt, bias=e[0:1]), deps=[hvv, epsh])
        if not double:
            hrs = P.emit("vector", lambda v: v.reciprocal(rrs, rtmp),
                         deps=[hsq])
        else:
            h_rs2 = P.emit("vector", lambda v: v.reciprocal(rrs, rtmp),
                           deps=[hsq])
            hv1 = P.emit("vector", lambda v: v.tensor_mul(rtmp, rex2, rrs),
                         deps=[h_rs2])
            hv2 = P.emit("vector", lambda v: v.tensor_mul(rtmp, rtmp, rrs),
                         deps=[hv1])
            hsq1 = P.emit("scalar", lambda s, e=epsc: s.activation(
                out=rtmp, in_=rtmp, func=AF.Sqrt, bias=e[0:1]), deps=[hv2])
            hr1 = P.emit("vector", lambda v: v.reciprocal(rex2, rtmp), deps=[hsq1])
            hrs = P.emit("vector", lambda v: v.tensor_mul(rrs, rrs, rex2),
                         deps=[hr1])
        if tb_hook is not None:
            tb_hook(tb, rm, rrs, hm, hrs)
        hbm = P.emit("tensor", lambda t, m=pb_m: t.matmul(k.PB[m], k.ONESROW, rm,
                                                          start=True, stop=True),
                     deps=[hm, hv, k.h_onesrow0])
        hbrs = P.emit("tensor", lambda t, r=pb_v: t.matmul(k.PB[r], k.ONESROW, rrs,
                                                           start=True, stop=True),
                      deps=[hrs, k.h_onesrow0])
        prev_hrs = hrs
        staged.append((tb, hbm, hbrs, pb_m, pb_v))
        if len(staged) > 1:
            emit_applies(*staged.pop(0))
    while staged:
        emit_applies(*staged.pop(0))
    return done


def proj(k, W_dram, n_in_ch, mc_list, rhs_fn, evac_fn, scale_cols=None,
         tb_list=None, psb=(0, 1, 2, 3), w_deps=(), psum_init=None):
    P = k.P
    tbs = tb_list if tb_list is not None else list(range(NTB))
    half = 0
    big = n_in_ch * 128 > 1024
    for mc in mc_list:
        if big:
            wdst = k.WST[:, 0:n_in_ch * 128]
            wb = k.WB[:, 0:n_in_ch * 128]
            wl_deps = [k.wst_last[0], k.wst_last[1]]
        else:
            wdst = k.WST[:, half * 1024: half * 1024 + n_in_ch * 128]
            wb = k.WB[:, half * 1024:(half + 1) * 1024]
            wl_deps = [k.wst_last[half]]
        hload = P.dma("sync", wdst, W_dram(mc), deps=list(w_deps) + wl_deps)
        casts = []
        for ci in range(n_in_ch):
            a = wb[:, ci * 128:(ci + 1) * 128]
            s_ = wdst[:, ci * 128:(ci + 1) * 128]
            wb_deps = [k.wb_last[0], k.wb_last[1]] if big else [k.wb_last[half]]
            if scale_cols is None:
                casts.append(P.emit("scalar",
                                    lambda e, a=a, s=s_: e.activation(
                                        out=a, in_=s, func=AF.Identity),
                                    deps=[hload] + wb_deps))
            else:
                scol, sdeps = scale_cols(ci)
                casts.append(P.emit("scalar",
                                    lambda e, a=a, s=s_, sc=scol: e.activation(
                                        out=a, in_=s, func=AF.Identity, scale=sc),
                                    deps=[hload] + wb_deps + list(sdeps)))
        if big:
            k.wst_last[0] = casts[-1]
            k.wst_last[1] = casts[-1]
        else:
            k.wst_last[half] = casts[-1]
        inits = {}
        if psum_init is not None:
            for j, tb in enumerate(tbs):
                pbi = psb[j % len(psb)]
                inits[tb] = psum_init(mc, tb, k.PB[pbi],
                                      [k.pb_last[pbi]] if k.pb_last[pbi] else [])
        mm_last = None
        for ci in range(n_in_ch):
            for j, tb in enumerate(tbs):
                rhs, rdeps = rhs_fn(ci, tb)
                pbi = psb[j % len(psb)]
                st = (ci == 0) and (psum_init is None)
                mm_last = P.emit("tensor",
                                 lambda t, pb=k.PB[pbi], w=wb[:, ci * 128:(ci + 1) * 128],
                                 r=rhs, st=st, sp=(ci == n_in_ch - 1):
                                 t.matmul(pb, w, r, start=st, stop=sp),
                                 deps=[casts[ci]] + list(rdeps) +
                                      ([inits[tb] if psum_init is not None
                                        else k.pb_last[pbi]] if ci == 0 else []))
        if big:
            k.wb_last[0] = mm_last
            k.wb_last[1] = mm_last
        else:
            k.wb_last[half] = mm_last
        k.last_tensor = mm_last
        for j, tb in enumerate(tbs):
            pbi = psb[j % len(psb)]
            h = evac_fn(mc, tb, k.PB[pbi], [mm_last])
            k.pb_last[pbi] = h
        half ^= 1


def build_batch(k, b, n_stages):
    P = k.P
    k.cur_batch = b
    k.xdeps = [P.dma("sync", xch(k, c), k.xin[b, c], deps=k.prev_stores)
               for c in range(NCH)]
    k.x_last = {(c, tb): k.xdeps[c] for c in range(NCH) for tb in range(NTB)}
    k.smb_last = None
    k.smf_last = None
    stage_l0_attn(k, b)
    if n_stages >= 2:
        stage_l0_ffn(k, b)
    if n_stages >= 3:
        stage_l1_attn(k, b)
    if n_stages >= 4:
        stage_l1_ffn(k, b)


def stage_l0_attn(k, b):
    P = k.P
    hdone = ln_pass(k, k.xdeps, lambda c, tb: (hch(k, c, tb), ()), bct=k.Y[:, 0:1536])
    hdep = lambda c: [hdone[(c, tb)] for tb in range(NTB)]

    hspan = P.dma("scalar", k.SMF[0:1, 0:16], k.dw["l0_span"], deps=k.prev_stores)
    hspb = P.dma("scalar", k.SMF[:, 16:32],
                 k.SMF[0:1, 0:16].unsqueeze(1).broadcast_to([1, 128, 16]), deps=[hspan])
    hm1 = P.emit("vector", lambda v: v.scalar_tensor_tensor(
        out=k.SMF[:, 32:48], in0=k.SMF[:, 16:32], scalar=float(S),
        in1=k.COLS[:, 1:2].broadcast_to([128, 16]), op0=ALU.mult, op1=ALU.add),
        deps=[hspb] + k.h_consts)
    hm2 = P.emit("vector", lambda v: v.tensor_scalar(
        out=k.SMF[:, 32:48], in0=k.SMF[:, 32:48], scalar1=1.0 / 33.0, scalar2=0.0,
        op0=ALU.mult, op1=ALU.max), deps=[hm1])
    hmask = P.emit("vector", lambda v: v.tensor_scalar_min(
        k.COLS[:, k.C_MASK16:k.C_MASK16 + 16], k.SMF[:, 32:48], 1.0), deps=[hm2])
    hmaskp = P.emit("vector", lambda v: v.tensor_scalar_add(
        k.COLS[:, k.C_MASKP16:k.C_MASKP16 + 16],
        k.COLS[:, k.C_MASK16:k.C_MASK16 + 16], 1e-8), deps=[hmask])
    if DEBUG and b == 0:
        hdc = P.dma("gpsimd", k.dbg_cols2, k.COLS, deps=[hmaskp])
        hds = P.dma("gpsimd", k.dbg_smf, k.SMF[:, 0:48], deps=[hmaskp])
        k.dbg_done += [hdc, hds]

    vplane = k.Yb[:, 0:16384]
    wv_store = k.Yb[:, 24576:32768]
    wv_h = []
    half = 0
    for ci in range(NCH):
        wdst = k.WST[:, half * 1024:(half + 1) * 1024]
        hload = P.dma("sync", wdst, k.dw["l0_wv"][ci * 128:(ci + 1) * 128, :],
                      deps=[k.wst_last[half]])
        d = wv_store[:, ci * 1024:(ci + 1) * 1024]
        hc = P.emit("scalar", lambda e, d=d, s=wdst: e.activation(
            out=d, in_=s, func=AF.Identity), deps=[hload])
        k.wst_last[half] = hc
        wv_h.append(hc)
        half ^= 1
    vready = []
    for tt in range(16):
        mm = [None, None]
        for ci in range(NCH):
            hstat = hch(k, ci)[:, tt * 128:(tt + 1) * 128]
            for j in range(2):
                w = wv_store[:, ci * 1024 + j * 512: ci * 1024 + (j + 1) * 512]
                mm[j] = P.emit("tensor",
                               lambda t, pb=k.PB[j], hs=hstat, w=w,
                               st=(ci == 0), sp=(ci == NCH - 1):
                               t.matmul(pb, hs, w, start=st, stop=sp),
                               deps=[wv_h[ci], hdone[(ci, tt // 4)]] +
                                    ([k.pb_last[j]] if ci == 0 else []))
        for j in range(2):
            dst = vplane[:, tt * 1024 + j * 512: tt * 1024 + (j + 1) * 512]
            h = P.emit("vector", lambda v, d=dst, p=k.PB[j]: v.tensor_copy(out=d, in_=p),
                       deps=[mm[j]])
            vready.append(h)
            k.pb_last[j] = h
    k.last_tensor = mm[1]

    QB = k.SC[:, 0:2048]
    KB = k.SC[:, 2048:4096]
    PBUF = k.SC[:, 4096:6144]
    aplane = k.Yb[:, 16384:32768]
    k.slot_last = [None] * 4
    att_all = []
    for pr in range(NPAIR):
        qk_ready = []
        for wd, buf in (("l0_wq", QB), ("l0_wk", KB)):
            def evac(mc, tb, psum, deps, buf=buf):
                dst = buf[:, tb * TB:(tb + 1) * TB]
                h = k.P.emit("vector",
                             lambda v, d=dst, p=psum: v.tensor_copy(out=d, in_=p),
                             deps=deps + [k.last_tensor])
                qk_ready.append(h)
                return h
            proj(k, lambda mc, wd=wd: k.dw[wd + "_s"][mc], NCH, [pr],
                 lambda ci, tb: (hch(k, ci, tb), [hdone[(ci, tb)]]),
                 evac, psb=(0, 1, 2, 3))

        for hh in (2 * pr, 2 * pr + 1):
            sta_h = {}
            for kc in range(16):
                vsl = vplane[:, kc * 1024 + hh * 64: kc * 1024 + hh * 64 + 64]
                dst = k.STA[:, kc * 66:kc * 66 + 64]
                dcol = k.STA[:, kc * 66 + 64:kc * 66 + 65]
                if kc == 0:
                    hv = P.emit("vector", lambda v, d=dst, s=vsl, hh=hh:
                                v.tensor_scalar_mul(
                                    d, s, k.COLS[:, k.C_MASK16 + hh:k.C_MASK16 + hh + 1]),
                                deps=vready + [hmask, k.last_tensor])
                    hc = P.emit("vector", lambda v, d=dcol, hh=hh: v.tensor_copy(
                        out=d, in_=k.COLS[:, k.C_MASKP16 + hh:k.C_MASKP16 + hh + 1]),
                        deps=[hmaskp, k.last_tensor])
                else:
                    hv = P.emit("vector", lambda v, d=dst, s=vsl: v.tensor_copy(out=d, in_=s),
                                deps=vready + [k.last_tensor])
                    hc = P.emit("vector", lambda v, d=dcol: v.tensor_copy(
                        out=d, in_=k.COLS[:, k.C_ONEP:k.C_ONEP + 1]),
                        deps=[k.h_onep, k.last_tensor])
                sta_h[kc] = (hv, hc)
            att_all += attention_head_l0(k, hh, QB, KB, PBUF, sta_h, aplane, qk_ready)

    if DEBUG:
        bar = P.barrier_handles()
        d1 = P.dma("gpsimd", k.dbg_h[b], k.Hb, deps=bar)
        d2 = P.dma("gpsimd", k.dbg_a[b], aplane, deps=bar)
        d3 = P.dma("gpsimd", k.dbg_v[b], vplane, deps=bar)
        P.emit("vector", lambda v: v.nop(), deps=[d1, d2, d3] + k.dbg_done)

    def evac_wo(mc, tb, psum, deps):
        dst = xch(k, mc, tb)
        h = P.emit("vector", lambda v, d=dst, p=psum: v.tensor_add(d, d, p),
                   deps=deps + [k.x_last[(mc, tb)]])
        k.x_last[(mc, tb)] = h
        return h
    proj(k, lambda mc: k.dw["l0_wo_s"][mc], NCH, list(range(NCH)),
         lambda ci, tb: (aplane[:, ci * S + tb * TB: ci * S + (tb + 1) * TB],
                         att_all[-1:] + [k.last_tensor]),
         evac_wo, psb=(0, 1, 2, 3))


def attention_head_l0(k, hh, QB, KB, PBUF, sta_h, aplane, qk_ready):
    P = k.P
    p0, p1 = (0, 64) if hh % 2 == 0 else (64, 128)
    done = []
    sci = 0
    for qb in range(NTB):
        kc_max = 4 * qb + 4
        pvb = 4 + (qb % 2)
        pv_pb = k.PB[pvb]
        pv_h = None
        pend = []

        def emit_pv(kc, hp, slot, scb, kc_max=kc_max, pv_pb=pv_pb, pvb=pvb):
            nonlocal pv_h
            sv, sc_ = sta_h[kc]
            pv_h = P.emit("tensor", lambda t, pb=pv_pb, l=k.STA[:, kc * 66:kc * 66 + 65],
                          r=slot, st=(kc == 0), sp=(kc == kc_max - 1):
                          t.matmul(pb[0:65, :], l, r, start=st, stop=sp),
                          deps=[hp, sv, sc_] + ([k.pb_last[pvb]] if kc == 0 else []))
            k.slot_last[scb] = pv_h
        for kc in range(kc_max):
            scb = sci % 4
            slot = PBUF[:, scb * 512:(scb + 1) * 512]
            sci += 1
            lhsT = KB[p0:p1, kc * 128:(kc + 1) * 128]
            rhs = QB[p0:p1, qb * TB:(qb + 1) * TB]
            hmm = P.emit("tensor", lambda t, pb=k.PB[scb], l=lhsT, r=rhs:
                         t.matmul(pb, l, r, start=True, stop=True),
                         deps=qk_ready + [k.pb_last[scb]])
            sc64, sc64h = k.constcol(1.0 / 64.0)
            hexp = P.emit("scalar", lambda s, o=slot, i=k.PB[scb], sc=sc64:
                          s.activation(out=o, in_=i, func=AF.Exp, scale=sc),
                          deps=[hmm, k.slot_last[scb], sc64h])
            k.pb_last[scb] = hexp
            hp = hexp
            off = kc * 128 - qb * TB
            if off >= 0:
                dsl = k.DIAGM[:, 384 - off:896 - off]
                hp = P.emit("vector", lambda v, o=slot, d=dsl: v.tensor_mul(o, o, d),
                            deps=[hexp] + k.h_consts)
            pend.append((kc, hp, slot, scb))
            if len(pend) > 2:
                emit_pv(*pend.pop(0))
        while pend:
            emit_pv(*pend.pop(0))
        k.last_tensor = pv_h
        if DEBUG and hh == 0 and qb == 3 and getattr(k, "cur_batch", 0) == 0:
            hd1 = P.emit("vector", lambda v, p=pv_pb: v.tensor_copy(
                out=k.SMF[0:65, 0:512], in_=p[0:65, :]), deps=[pv_h])
            hd2 = P.dma("gpsimd", k.dbg_pv, k.SMF[:, 0:512], deps=[hd1])
            hd3 = P.dma("gpsimd", k.dbg_p, PBUF, deps=[pv_h])
            hd4 = P.dma("gpsimd", k.dbg_sta, k.STA, deps=[pv_h])
            hd5 = P.dma("gpsimd", k.dbg_cols, k.COLS, deps=[pv_h])
            k.dbg_done += [hd2, hd3, hd4, hd5]
        hrc = P.emit("vector", lambda v, p=pv_pb: v.reciprocal(k.SMF[64:65, 0:512],
                                                               p[64:65, :]),
                     deps=[pv_h] + (k.dbg_done[-3:-2] if (DEBUG and hh == 0 and qb == 3) else []))
        hbc = P.emit("tensor", lambda t: t.matmul(k.PB[6][0:64, :], k.SMF[64:65, 512:576],
                                                  k.SMF[64:65, 0:512],
                                                  start=True, stop=True),
                     deps=[hrc, k.h_onesrow64, k.pb_last[6]])
        k.last_tensor = hbc
        hcp = P.emit("vector", lambda v: v.tensor_copy(out=k.SMF[0:64, 0:512],
                                                       in_=k.PB[6][0:64, :]),
                     deps=[hbc])
        k.pb_last[6] = hcp
        c = hh // 2
        rows = slice((hh % 2) * 64, (hh % 2) * 64 + 64)
        dst = aplane[rows, c * S + qb * TB: c * S + (qb + 1) * TB]
        ha = P.emit("vector", lambda v, d=dst, p=pv_pb: v.tensor_tensor(
            d, p[0:64, :], k.SMF[0:64, 0:512], ALU.mult), deps=[hcp, pv_h])
        k.pb_last[pvb] = ha
        done.append(ha)
    return done


def _make_in_map(x, cons, w):
    m = {}
    for nm, v in cons.items():
        m["c_" + nm] = v
    for nm, v in w.items():
        m["w_" + nm] = v
    return m


def kernel(**inputs):
    cons = host_constants()
    cons = {nm: np.ascontiguousarray(v) for nm, v in cons.items()}
    w = prep_weights({kk: np.asarray(vv) for kk, vv in inputs.items()})
    w = {nm: np.ascontiguousarray(v.astype(np.float32, copy=False)) for nm, v in w.items()}
    x = np.asarray(inputs["L0_attn_query"])
    nc = build(N_STAGES, NB)
    base = _make_in_map(x, cons, w)
    in_maps = []
    for core in range(8):
        m = dict(base)
        xs = x[2 * core:2 * core + 2].transpose(0, 2, 1)
        m["xin"] = np.ascontiguousarray(xs.reshape(NB, NCH, 128, S))
        in_maps.append(m)
    res = run_bass_kernel_spmd(nc, in_maps, core_ids=list(range(8)))
    out = np.empty((16, S, D), dtype=np.float32)
    for core in range(8):
        o = np.asarray(res.results[core]["xout"]).reshape(NB, D, S)
        out[2 * core:2 * core + 2] = o.transpose(0, 2, 1)
    return out


def stage_l0_ffn(k, b):
    P = k.P
    EPS = 0.1
    R = k.ROUT
    SR, UU, SCR = R[:, 0:16], R[:, 16:32], R[:, 32:48]
    RANK, SEL = R[:, 48:64], R[:, 64:80]
    NACOL, MCOL, BCOL = R[:, 81:82], R[:, 82:83], R[:, 83:84]
    SMAXC, RSUM = R[:, 84:85], R[:, 85:86]
    CEL = k.SMF[0:1, 640:768]
    c_a, c_R, c_lnS, c_M, c_smax = (CEL[:, i:i + 1] for i in range(5))
    ONESR0 = k.SMF[0:1, 512:640]
    h_or0 = P.emit("vector", lambda v: v.memset(ONESR0, 1.0))

    front2 = [k.last_tensor]
    h_rtc = P.dma("sync", k.SMF[:, 48:56], k.dw["rt_col"], deps=front2)
    h_gl = P.dma("sync", k.SMF[:, 56:64], k.dw["gam_l"], deps=front2)
    h_gh = P.dma("sync", k.SMF[:, 64:72], k.dw["gam_h"], deps=front2)
    h_b1l = P.dma("sync", k.BIASC[:, 0:4], k.dw["b1l"], deps=front2)
    h_b2l = P.dma("sync", k.BIASC[:, 4:12], k.dw["b2l"], deps=front2)
    h_b1h = P.dma("sync", k.BIASC[:, 12:44], k.dw["b1h"], deps=front2)
    RTC, GLC, GHC = k.SMF[:, 48:56], k.SMF[:, 56:64], k.SMF[:, 64:72]

    h_R0 = P.emit("tensor", lambda t: t.matmul(k.PB[2][0:1, 0:8], k.COLS[:, 0:1], RTC,
                                               start=True, stop=True),
                  deps=[h_rtc, k.h_ones, k.pb_last[2]])
    h_R = P.emit("vector", lambda v: v.tensor_reduce(c_R, k.PB[2][0:1, 0:8],
                                                     axis=mybir.AxisListType.X, op=ALU.add),
                 deps=[h_R0])
    k.pb_last[2] = h_R

    s_spread = []

    def hook(tb, rm, rrs, hm, hrs):
        hmm = None
        for c in range(NCH):
            hmm = P.emit("tensor", lambda t, c=c, tb=tb, st=(c == 0), sp=(c == NCH - 1):
                         t.matmul(k.PB[3][0:1, :], RTC[:, c:c + 1], xch(k, c, tb),
                                  start=st, stop=sp),
                         deps=[h_rtc, k.x_last[(c, tb)]] +
                              ([k.pb_last[3]] if c == 0 else []))
        rex2 = k.ROW[0:1, 1024:1536]
        rtmp = k.ROW[0:1, 1536:2048]
        h1 = P.emit("vector", lambda v: v.tensor_scalar(out=rex2, in0=rm, scalar1=c_R,
                                                        scalar2=None, op0=ALU.mult),
                    deps=[hm, h_R, hrs])
        h2 = P.emit("vector", lambda v: v.tensor_sub(rtmp, k.PB[3][0:1, :], rex2),
                    deps=[h1, hmm])
        k.pb_last[3] = h2
        h3 = P.emit("vector", lambda v: v.tensor_mul(rtmp, rtmp, rrs), deps=[h2])
        hsp = P.dma("scalar", SR[32 * tb:32 * tb + 32, :], rtmp, deps=[h3])
        s_spread.append(hsp)

    hdone = ln_pass(k, k.xdeps, lambda c, tb: (hch(k, c, tb), ()),
                    bct=k.Y[:, 8192:9728], tb_hook=hook,
                    src_fn=lambda c, tb: (xch(k, c, tb), [k.x_last[(c, tb)]]))

    h_m1 = P.emit("vector", lambda v: v.tensor_reduce(RSUM, SR, axis=mybir.AxisListType.X,
                                                      op=ALU.max), deps=s_spread)
    h_m2 = P.dma("scalar", k.ROW[0:1, 0:128], RSUM, deps=[h_m1])
    h_m3 = P.emit("vector", lambda v: v.tensor_reduce(c_smax, k.ROW[0:1, 0:128],
                                                      axis=mybir.AxisListType.X, op=ALU.max),
                  deps=[h_m2])
    h_m4 = P.dma("scalar", SMAXC, c_smax.unsqueeze(1).broadcast_to([1, 128, 1]),
                 deps=[h_m3])

    EXPC, EXPH = k.constcol(1.0 / EPS)
    const_ab = float(EPS * np.log(8.0))
    h_a = P.emit("vector", lambda v: v.memset(c_a, 0.0))
    h_nacol = P.emit("vector", lambda v: v.memset(NACOL, 0.0))
    prev = [h_a, h_nacol, h_m4, h_m3] + s_spread
    for it in range(50):
        h1 = P.emit("vector", lambda v: v.tensor_tensor(MCOL, SMAXC, NACOL, ALU.min),
                    deps=prev)
        h2 = P.emit("vector", lambda v: v.tensor_scalar_mul(BCOL, MCOL, -1.0 / EPS),
                    deps=[h1])
        h3 = P.emit("vector", lambda v: v.tensor_scalar(out=UU, in0=SR, scalar1=NACOL,
                                                        scalar2=None, op0=ALU.min),
                    deps=prev)
        h4 = P.emit("scalar", lambda sc: sc.activation(out=UU, in_=UU, func=AF.Exp,
                                                       bias=BCOL, scale=EXPC),
                    deps=[h3, h2, EXPH])
        h5 = P.emit("tensor", lambda t: t.matmul(k.PB[3][0:1, 0:16], k.COLS[:, 0:1],
                                                 UU, start=True, stop=True),
                    deps=[h4, k.h_ones, k.pb_last[3]])
        h6b = P.emit("vector", lambda v: v.tensor_reduce(CEL[:, 5:6],
                                                         k.PB[3][0:1, 0:16],
                                                         axis=mybir.AxisListType.X,
                                                         op=ALU.add), deps=[h5])
        k.pb_last[3] = h6b
        h7 = P.emit("scalar", lambda sc: sc.activation(out=c_lnS, in_=CEL[:, 5:6],
                                                       func=AF.Ln), deps=[h6b])
        h8 = P.emit("vector", lambda v: v.tensor_copy(out=c_M, in_=MCOL[0:1, :]),
                    deps=[h1])
        h9 = P.emit("vector", lambda v: v.tensor_scalar(out=c_a, in0=c_lnS, scalar1=-EPS,
                                                        scalar2=None, op0=ALU.mult),
                    deps=[h7])
        h10 = P.emit("vector", lambda v: v.tensor_sub(c_a, c_a, c_M), deps=[h9, h8])
        h11 = P.emit("vector", lambda v: v.tensor_scalar_add(c_a, c_a, const_ab),
                     deps=[h10])
        h12 = P.emit("tensor", lambda t: t.matmul(k.PB[2][:, 0:1], k.ONESROW, c_a,
                                                  start=True, stop=True),
                     deps=[h11, k.h_onesrow0, k.pb_last[2]])
        h13 = P.emit("vector", lambda v: v.tensor_scalar_mul(NACOL, k.PB[2][:, 0:1],
                                                             -1.0), deps=[h12])
        k.pb_last[2] = h13
        prev = [h13, h11]
    hf1 = P.emit("vector", lambda v: v.tensor_scalar(out=UU, in0=SR,
                                                     scalar1=NACOL,
                                                     scalar2=0.0, op0=ALU.subtract,
                                                     op1=ALU.min),
                 deps=prev)
    hf2 = P.emit("scalar", lambda sc: sc.activation(out=SCR, in_=UU, func=AF.Exp,
                                                    scale=EXPC), deps=[hf1, EXPH])
    hf3 = P.emit("vector", lambda v: v.tensor_scalar(out=SCR, in0=SCR,
                                                     scalar1=float(2.0 ** -24),
                                                     scalar2=float(2.0 ** 24),
                                                     op0=ALU.mult, op1=ALU.mult),
                 deps=[hf2])
    h_sb = P.emit("vector", lambda v: v.tensor_copy(out=k.SMB[:, 0:16], in_=SCR),
                  deps=[hf3])
    SCR2 = R[:, 86:96]
    SCR2 = k.ROUT[:, 16:32]
    h_sb2 = P.emit("vector", lambda v: v.tensor_copy(out=SCR2, in_=k.SMB[:, 0:16]),
                   deps=[h_sb])
    h_row = P.dma("scalar", k.Yb[0:1, 24576:26624], k.SMB[:, 0:16], deps=[h_sb])
    BCR = k.Yb[:, 26624:28672]
    h_bc = P.dma("scalar", BCR,
                 k.Yb[0:1, 24576:26624].unsqueeze(1).broadcast_to([1, 128, 2048]),
                 deps=[h_row])
    GT = k.Yb[:, 28672:30720]
    EQ = k.Yb[:, 30720:32768]
    h_rank = []
    for c in range(16):
        g1 = P.emit("vector", lambda v, c=c: v.tensor_scalar(out=GT, in0=BCR,
                                                             scalar1=SCR2[:, c:c + 1],
                                                             scalar2=None, op0=ALU.is_gt),
                    deps=[h_bc, h_sb2] + h_rank[-1:])
        g2 = P.emit("vector", lambda v, c=c: v.tensor_scalar(out=EQ, in0=BCR,
                                                             scalar1=SCR2[:, c:c + 1],
                                                             scalar2=None,
                                                             op0=ALU.is_equal),
                    deps=[h_bc, h_sb2] + h_rank[-1:])
        g3 = P.emit("vector", lambda v, c=c: v.tensor_mul(EQ, EQ,
                                                          k.STAIR2[:, 16 - c:2064 - c]),
                    deps=[g2] + k.h_consts)
        g4 = P.emit("vector", lambda v: v.tensor_add(GT, GT, EQ), deps=[g1, g3])
        g5 = P.emit("vector", lambda v, c=c: v.tensor_reduce(RANK[:, c:c + 1], GT,
                                                             axis=mybir.AxisListType.X,
                                                             op=ALU.add), deps=[g4])
        h_rank.append(g5)
    h_sel = P.emit("vector", lambda v: v.tensor_scalar(out=SEL, in0=RANK, scalar1=1024.0,
                                                       scalar2=None, op0=ALU.is_lt),
                   deps=h_rank)
    h_selb16 = P.emit("vector", lambda v: v.tensor_copy(out=k.SMB[:, 16:32], in_=SEL),
                      deps=[h_sel])
    h_selrow = P.dma("scalar", k.Yb[0:1, 24576:26624], k.SMB[:, 16:32], deps=[h_selb16])
    SELB = k.Yb[:, 26624:28672]
    h_selbc = P.dma("scalar", SELB,
                    k.Yb[0:1, 24576:26624].unsqueeze(1).broadcast_to([1, 128, 2048]),
                    deps=[h_selrow])

    nplane = k.Yb[:, 0:16384]
    XSQ = k.SMB[:, 512:1024]
    ndone = {}
    hb = None
    for tb in range(NTB):
        hq = None
        sq_prev = None
        for c in range(NCH):
            sq_prev = P.emit("scalar", lambda sc, c=c, tb=tb: sc.activation(
                out=XSQ, in_=hch(k, c, tb), func=AF.Square),
                deps=[hdone[(c, tb)], hq])
            hq = P.emit("tensor", lambda t, st=(c == 0), sp=(c == NCH - 1):
                        t.matmul(k.PB[6][0:1, :], k.COLB[:, 0:1], XSQ, start=st, stop=sp),
                        deps=[sq_prev, k.h_onesb, k.pb_last[6] if c == 0 else None])
        rtmp = k.ROW[0:1, 1536:2048]
        hs1 = P.emit("scalar", lambda sc: sc.activation(out=rtmp, in_=k.PB[6][0:1, :],
                                                        func=AF.Sqrt), deps=[hq, hb])
        k.pb_last[6] = hs1
        hs2 = P.emit("vector", lambda v: v.tensor_scalar_max(rtmp, rtmp, 1e-12),
                     deps=[hs1])
        hs3 = P.emit("vector", lambda v: v.reciprocal(rtmp, rtmp), deps=[hs2])
        hs4 = P.emit("vector", lambda v: v.tensor_scalar_mul(rtmp, rtmp,
                                                             float(np.sqrt(D))),
                     deps=[hs3])
        BCN = k.Y[:, 9728:10240]
        hb = P.dma("scalar", BCN, rtmp.unsqueeze(1).broadcast_to([1, 128, 512]),
                   deps=[hs4] + [ndone.get((c, tb - 1)) for c in range(NCH)])
        for c in range(NCH):
            dst = nplane[:, c * 2048 + tb * 512: c * 2048 + (tb + 1) * 512]
            hn = P.emit("vector", lambda v, dst=dst, c=c, tb=tb: v.tensor_mul(
                dst, hch(k, c, tb), BCN), deps=[hb, hdone[(c, tb)]])
            ndone[(c, tb)] = hn

    def nch(ci, tb):
        return (nplane[:, ci * 2048 + tb * 512: ci * 2048 + (tb + 1) * 512],
                [ndone[(ci, tb)]])

    TL = lambda m, tb: k.Yb[:, 16384 + m * 2048 + tb * 512: 16384 + m * 2048 + (tb + 1) * 512]
    tl_done = {}

    def evac_t(mc, tb, psum, deps):
        dst = TL(mc, tb)
        h = P.emit("scalar", lambda sc, dst=dst, p=psum, mc=mc: sc.activation(
            out=dst, in_=p, func=AF.Gelu, bias=k.BIASC[:, mc:mc + 1]),
            deps=deps + [h_b1l])
        tl_done[(mc, tb)] = h
        return h
    proj(k, lambda mc: k.dw["w1l_s"][mc], NCH, [0, 1, 2, 3], nch, evac_t,
         scale_cols=lambda ci: (GLC[:, ci:ci + 1], [h_gl]), psb=(0, 1, 2, 3))

    def evac_xl(mc, tb, psum, deps):
        dst = xch(k, mc, tb)
        h = P.emit("vector", lambda v, d=dst, p=psum, mc=mc:
                   v.scalar_tensor_tensor(out=d, in0=p,
                                          scalar=k.BIASC[:, 4 + mc:5 + mc],
                                          in1=d, op0=ALU.add, op1=ALU.add),
                   deps=deps + [h_b2l, k.x_last[(mc, tb)]])
        k.x_last[(mc, tb)] = h
        return h
    proj(k, lambda mc: k.dw["w2l_s"][mc], 4, list(range(NCH)),
         lambda ci, tb: (TL(ci, tb), [tl_done[(ci, tb)]]),
         evac_xl, psb=(0, 1, 2, 3))

    h_b2hf = P.dma("sync", k.ROW[0:1, 0:1024], k.dw["b2h_row"], deps=[hf3])
    h_b2hb = P.emit("vector", lambda v: v.tensor_copy(out=k.SMB[0:1, 0:1024],
                                                      in_=k.ROW[0:1, 0:1024]),
                    deps=[h_b2hf, h_sb])
    for hc in range(8):
        th_done = {}

        def evac_th(mc, tb, psum, deps, hc=hc):
            m = mc - 4 * hc
            dst = TL(m, tb)
            h1 = P.emit("scalar", lambda sc, dst=dst, p=psum, mc=mc: sc.activation(
                out=dst, in_=p, func=AF.Gelu, bias=k.BIASC[:, 12 + mc:13 + mc]),
                deps=deps + [h_b1h])
            h2 = P.emit("vector", lambda v, dst=dst, tb=tb: v.tensor_mul(
                dst, dst, SELB[:, tb * 512:(tb + 1) * 512]), deps=[h1, h_selbc])
            th_done[(m, tb)] = h2
            return h2
        proj(k, lambda mc: k.dw["w1h_s"][mc], NCH, [4 * hc, 4 * hc + 1, 4 * hc + 2, 4 * hc + 3],
             nch, evac_th, scale_cols=lambda ci: (GHC[:, ci:ci + 1], [h_gh]),
             psb=(0, 1, 2, 3))

        def evac_xh(mc, tb, psum, deps):
            dst = xch(k, mc, tb)
            h = P.emit("vector", lambda v, d=dst, p=psum: v.tensor_add(d, d, p),
                       deps=deps + [k.x_last[(mc, tb)]])
            k.x_last[(mc, tb)] = h
            return h
        proj(k, lambda mc, hc=hc: k.dw["w2h_s"][mc][:, hc * 512:(hc + 1) * 512], 4, list(range(NCH)),
             lambda ci, tb: (TL(ci, tb), [th_done[(ci, tb)]]),
             evac_xh, psb=(0, 1, 2, 3),
             psum_init=(None if hc != 0 else
                        (lambda mc, tb, pb, deps2: P.emit(
                            "tensor", lambda t, pb=pb, mc=mc, tb=tb: t.matmul(
                                pb, k.SMB[0:1, mc * 128:(mc + 1) * 128],
                                SELB[0:1, tb * 512:(tb + 1) * 512],
                                start=True, stop=False),
                            deps=deps2 + [h_b2hb, h_selbc]))))


def stage_l1_attn(k, b):
    P = k.P
    vplane = k.Yb[:, 0:16384]
    ATT = k.Yb[:, 16384:32768]
    QB = k.SC[:, 0:2048]
    KB = k.SC[:, 2048:4096]
    SLA = k.SC[:, 4096:4608]
    SLB = k.SC[:, 4608:5120]
    ATBL = k.SMF[:, 0:512]
    BTBL = k.SMB.bitcast(F32)[:, 0:512]

    front3 = k.P.barrier_handles()
    h_l1w = P.dma("sync", k.SMF[:, 48:56], k.dw["l1_ln1_w"], deps=front3)
    h_l1b = P.dma("sync", k.SMF[:, 56:64], k.dw["l1_ln1_b"], deps=front3)
    h_l2w = P.dma("sync", k.BIASC[:, 52:60], k.dw["l1_ln2_w"], deps=front3)
    h_l2b = P.dma("sync", k.BIASC[:, 44:52], k.dw["l1_ln2_b"], deps=front3)
    h_b1 = P.dma("sync", k.BIASC[:, 0:8], k.dw["l1_b1"], deps=front3)
    h_b2 = P.dma("sync", k.BIASC[:, 8:16], k.dw["l1_b2"], deps=front3)
    h_gnw = P.dma("scalar", k.SMF[0:1, 672:688], k.dw["l1_gn_w"], deps=front3)
    h_gnb = P.dma("scalar", k.SMF[0:1, 688:704], k.dw["l1_gn_b"], deps=front3)

    hdone = ln_pass(k, k.xdeps, lambda c, tb: (hch(k, c, tb), ()),
                    bct=k.SCf[:, 0:1536], double=True,
                    src_fn=lambda c, tb: (xch(k, c, tb), [k.x_last[(c, tb)]]))
    h1done = {}
    for c in range(NCH):
        h = P.emit("vector", lambda v, c=c: v.tensor_scalar(
            out=hch(k, c), in0=hch(k, c), scalar1=k.SMF[:, 48 + c:49 + c],
            scalar2=k.SMF[:, 56 + c:57 + c], op0=ALU.mult, op1=ALU.add),
            deps=[hdone[(c, tb)] for tb in range(NTB)] + [h_l1w, h_l1b])
        for tb in range(NTB):
            h1done[(c, tb)] = h

    if DEBUG and b == 0:
        bar = P.barrier_handles()
        kd0 = P.dma("gpsimd", k.dbg_p, k.Hb[:, 0:2048], deps=bar)
        k.dbg_done += [kd0]
        dbg_fence(k, [kd0])

    half = 0
    wv_store = ATT[:, 0:8192]
    wv_h = []
    for ci in range(NCH):
        wdst = k.WST[:, half * 1024:(half + 1) * 1024]
        hload = P.dma("sync", wdst, k.dw["l1_wv"][ci * 128:(ci + 1) * 128, :],
                      deps=[k.wst_last[half]])
        d = wv_store[:, ci * 1024:(ci + 1) * 1024]
        hc = P.emit("scalar", lambda e, d=d, s=wdst: e.activation(
            out=d, in_=s, func=AF.Identity), deps=[hload, k.last_tensor])
        k.wst_last[half] = hc
        wv_h.append(hc)
        half ^= 1
    vready = []
    for tt in range(16):
        mm = [None, None]
        for ci in range(NCH):
            hstat = hch(k, ci)[:, tt * 128:(tt + 1) * 128]
            for j in range(2):
                w = wv_store[:, ci * 1024 + j * 512: ci * 1024 + (j + 1) * 512]
                mm[j] = P.emit("tensor",
                               lambda t, pb=k.PB[j], hs=hstat, w=w,
                               st=(ci == 0), sp=(ci == NCH - 1):
                               t.matmul(pb, hs, w, start=st, stop=sp),
                               deps=[wv_h[ci], h1done[(ci, tt // 4)]] +
                                    ([k.pb_last[j]] if ci == 0 else []))
        for j in range(2):
            dst = vplane[:, tt * 1024 + j * 512: tt * 1024 + (j + 1) * 512]
            h = P.emit("vector", lambda v, d=dst, p=k.PB[j]: v.tensor_copy(out=d, in_=p),
                       deps=[mm[j]])
            vready.append(h)
            k.pb_last[j] = h
    k.last_tensor = mm[1]

    SUMC = k.ROUT[:, 0:2]
    att_done = {}
    for pr in range(NPAIR):
        qk_ready = []
        for (wmain, wswap, tA, tB, buf) in (
                ("l1_wq", "l1_wqs", f"rotAq{pr}", f"rotBq{pr}", QB),
                ("l1_wk", "l1_wks", f"rotAk{pr}", f"rotBk{pr}", KB)):
            wdst0 = k.WST[:, 0:1024]
            hl0 = P.dma("sync", wdst0, k.dw[wmain + "_s"][pr],
                        deps=[k.wst_last[0]])
            wb0 = k.WB[:, 0:1024]
            hc0 = P.emit("scalar", lambda e, a=wb0, s=wdst0: e.activation(
                out=a, in_=s, func=AF.Identity), deps=[hl0, k.wb_last[0]])
            k.wst_last[0] = hc0
            wdst1 = k.WST[:, 1024:2048]
            hl1 = P.dma("sync", wdst1, k.dw[wswap + "_s"][pr],
                        deps=[k.wst_last[1]])
            wb1 = k.WB[:, 1024:2048]
            hc1 = P.emit("scalar", lambda e, a=wb1, s=wdst1: e.activation(
                out=a, in_=s, func=AF.Identity), deps=[hl1, k.wb_last[1]])
            k.wst_last[1] = hc1
            mmQ = mmS = None
            for tb in range(NTB):
                hA = P.dma("scalar", ATBL, k.dc[tA][:, tb * TB:(tb + 1) * TB],
                           deps=(qk_ready[-1:] or [k.last_tensor]) + [k.smf_last])
                hB = P.dma("scalar", BTBL, k.dc[tB][:, tb * TB:(tb + 1) * TB],
                           deps=(qk_ready[-1:] or [k.last_tensor]) + [k.smb_last])
                k.smb_last = hB
                k.smf_last = hA
                for ci in range(NCH):
                    mmQ = P.emit("tensor", lambda t, ci=ci, tb=tb,
                                 w=wb0[:, ci * 128:(ci + 1) * 128],
                                 st=(ci == 0), sp=(ci == NCH - 1):
                                 t.matmul(k.PB[6], w, hch(k, ci, tb), start=st, stop=sp),
                                 deps=[hc0, h1done[(ci, tb)]] +
                                      ([k.pb_last[6]] if ci == 0 else []))
                    mmS = P.emit("tensor", lambda t, ci=ci, tb=tb,
                                 w=wb1[:, ci * 128:(ci + 1) * 128],
                                 st=(ci == 0), sp=(ci == NCH - 1):
                                 t.matmul(k.PB[7], w, hch(k, ci, tb), start=st, stop=sp),
                                 deps=[hc1, h1done[(ci, tb)]] +
                                      ([k.pb_last[7]] if ci == 0 else []))
                k.last_tensor = mmS
                k.wb_last[0] = mmQ
                k.wb_last[1] = mmS
                sl = slice(tb * TB, (tb + 1) * TB)
                r1 = P.emit("vector", lambda v, sl=sl, buf=buf: v.tensor_tensor(
                    buf[:, sl], k.PB[6], ATBL, ALU.mult), deps=[mmQ, hA])
                r2 = P.emit("vector", lambda v: v.tensor_tensor(
                    SLA, k.PB[7], BTBL, ALU.mult),
                    deps=[mmS, hB, k.slot_last[0]])
                r3 = P.emit("vector", lambda v, sl=sl, buf=buf: v.tensor_add(
                    buf[:, sl], buf[:, sl], SLA), deps=[r1, r2])
                k.pb_last[6] = r1
                k.pb_last[7] = r2
                k.slot_last[0] = r3
                qk_ready.append(r3)
        for qb in range(NTB):
            kc_max = 4 * qb + 4
            pvb = 4 + (qb % 2)
            pv_pb = k.PB[pvb]
            pv_h = [None, None]
            sci = 0
            pend = []

            def emit_pv(kc, j, hh, hp, slot, scb, kc_max=kc_max, pv_pb=pv_pb,
                        pvb=pvb):
                vsl = vplane[:, kc * 1024 + hh * 64: kc * 1024 + hh * 64 + 64]
                pv_h[j] = P.emit("tensor", lambda t, pb=pv_pb, l=vsl, r=slot,
                                 st=(kc == 0), sp=(kc == kc_max - 1), j=j:
                                 t.matmul(pb[64 * j:64 * j + 64, :], l, r,
                                          start=st, stop=sp),
                                 deps=[hp] + vready +
                                      ([k.pb_last[pvb]] if (kc == 0 and j == 0) else []))
                k.slot_last[scb] = pv_h[j]
            for kc in range(kc_max):
                for j, hh in enumerate((2 * pr, 2 * pr + 1)):
                    p0, p1 = (0, 64) if j == 0 else (64, 128)
                    scb = sci % 2
                    slot = (SLA, SLB)[scb]
                    sci += 1
                    hmm = P.emit("tensor", lambda t, pb=k.PB[scb],
                                 l=KB[p0:p1, kc * 128:(kc + 1) * 128],
                                 r=QB[p0:p1, qb * TB:(qb + 1) * TB]:
                                 t.matmul(pb, l, r, start=True, stop=True),
                                 deps=qk_ready[-8:] + [k.pb_last[scb]])
                    off = kc * 128 - qb * TB
                    if off >= 0:
                        hp = P.emit("vector", lambda v, o=slot, pb=k.PB[scb],
                                    d=k.DIAGM[:, 384 - off:896 - off]:
                                    v.tensor_tensor(o, pb, d, ALU.mult),
                                    deps=[hmm, k.slot_last[scb]] + k.h_consts)
                    else:
                        hp = P.emit("vector", lambda v, o=slot, pb=k.PB[scb]:
                                    v.tensor_copy(out=o, in_=pb),
                                    deps=[hmm, k.slot_last[scb]])
                    k.pb_last[scb] = hp
                    pend.append((kc, j, hh, hp, slot, scb))
                    if len(pend) > 1:
                        emit_pv(*pend.pop(0))
            while pend:
                emit_pv(*pend.pop(0))
            k.last_tensor = pv_h[1]
            dst = ATT[:, pr * 2048 + qb * TB: pr * 2048 + (qb + 1) * TB]
            he = P.emit("vector", lambda v, d=dst, p=pv_pb: v.tensor_copy(out=d, in_=p),
                        deps=pv_h)
            k.pb_last[pvb] = he
            hsq = P.emit("scalar", lambda sc, d=dst: sc.activation(
                out=k.SMB[:, 0:512], in_=d, func=AF.Square), deps=[he, k.smb_last])
            hr1 = P.emit("vector", lambda v, d=dst, qb=qb: v.tensor_reduce(
                k.ROUT[:, 2:3], d, axis=mybir.AxisListType.X, op=ALU.add), deps=[he])
            hr2 = P.emit("vector", lambda v: v.tensor_reduce(
                k.ROUT[:, 3:4], k.SMB[:, 0:512], axis=mybir.AxisListType.X, op=ALU.add),
                deps=[hsq])
            k.smb_last = hr2
            if qb == 0:
                a1 = P.emit("vector", lambda v: v.tensor_copy(out=SUMC[:, 0:1],
                                                              in_=k.ROUT[:, 2:3]),
                            deps=[hr1])
                a2 = P.emit("vector", lambda v: v.tensor_copy(out=SUMC[:, 1:2],
                                                              in_=k.ROUT[:, 3:4]),
                            deps=[hr2])
            else:
                a1 = P.emit("vector", lambda v: v.tensor_add(SUMC[:, 0:1], SUMC[:, 0:1],
                                                             k.ROUT[:, 2:3]), deps=[hr1, a2])
                a2 = P.emit("vector", lambda v: v.tensor_add(SUMC[:, 1:2], SUMC[:, 1:2],
                                                             k.ROUT[:, 3:4]), deps=[hr2, a1])
            att_done[(pr, qb)] = he
        hd1 = P.dma("scalar", k.ROW[0:1, 0:128], SUMC[:, 0:1], deps=[a1, a2])
        hd2 = P.dma("scalar", k.ROW[0:1, 128:256], SUMC[:, 1:2], deps=[a1, a2])
        CB = k.SMF[0:1, 704:768]
        cS1 = CB[:, 0:2]
        cS2 = CB[:, 2:4]
        cM = CB[:, 4:6]
        cV = CB[:, 6:8]
        cA = CB[:, 8:10]
        cB2 = CB[:, 10:12]
        g1 = P.emit("vector", lambda v: v.tensor_reduce(cS1[:, 0:1], k.ROW[0:1, 0:64],
                                                        axis=mybir.AxisListType.X, op=ALU.add),
                    deps=[hd1])
        g2 = P.emit("vector", lambda v: v.tensor_reduce(cS1[:, 1:2], k.ROW[0:1, 64:128],
                                                        axis=mybir.AxisListType.X, op=ALU.add),
                    deps=[hd1])
        g3 = P.emit("vector", lambda v: v.tensor_reduce(cS2[:, 0:1], k.ROW[0:1, 128:192],
                                                        axis=mybir.AxisListType.X, op=ALU.add),
                    deps=[hd2])
        g4 = P.emit("vector", lambda v: v.tensor_reduce(cS2[:, 1:2], k.ROW[0:1, 192:256],
                                                        axis=mybir.AxisListType.X, op=ALU.add),
                    deps=[hd2])
        NI = 1.0 / (S * DH)
        g5 = P.emit("vector", lambda v: v.tensor_scalar_mul(cM, cS1, NI), deps=[g1, g2])
        g6 = P.emit("vector", lambda v: v.tensor_scalar_mul(cV, cS2, NI), deps=[g3, g4])
        g7 = P.emit("vector", lambda v: v.tensor_mul(cS1, cM, cM), deps=[g5])
        g8 = P.emit("vector", lambda v: v.tensor_sub(cV, cV, cS1), deps=[g6, g7])
        epsc, epsh = k.constcol(1e-5)
        g9 = P.emit("scalar", lambda sc: sc.activation(out=cV, in_=cV, func=AF.Sqrt,
                                                       bias=epsc[0:1]), deps=[g8, epsh])
        g10 = P.emit("vector", lambda v: v.reciprocal(cV, cV), deps=[g9])
        gnw = k.SMF[0:1, 672 + 2 * pr:674 + 2 * pr]
        gnb = k.SMF[0:1, 688 + 2 * pr:690 + 2 * pr]
        g11 = P.emit("vector", lambda v: v.tensor_mul(cA, cV, gnw), deps=[g10, h_gnw])
        g12 = P.emit("vector", lambda v: v.tensor_mul(cS2, cM, cA), deps=[g11, g5])
        g13 = P.emit("vector", lambda v: v.tensor_sub(cB2, gnb, cS2), deps=[g12, h_gnb])
        AC = k.ROUT[:, 4:5]
        BC = k.ROUT[:, 5:6]
        b1_ = P.dma("scalar", AC[0:64, :], cA[:, 0:1].unsqueeze(1).broadcast_to([1, 64, 1]),
                    deps=[g11])
        b2_ = P.dma("scalar", AC[64:128, :], cA[:, 1:2].unsqueeze(1).broadcast_to([1, 64, 1]),
                    deps=[g11])
        b3_ = P.dma("scalar", BC[0:64, :], cB2[:, 0:1].unsqueeze(1).broadcast_to([1, 64, 1]),
                    deps=[g13])
        b4_ = P.dma("scalar", BC[64:128, :], cB2[:, 1:2].unsqueeze(1).broadcast_to([1, 64, 1]),
                    deps=[g13])
        for qb in range(NTB):
            dst = ATT[:, pr * 2048 + qb * TB: pr * 2048 + (qb + 1) * TB]
            hn = P.emit("vector", lambda v, d=dst: v.tensor_scalar(
                out=d, in0=d, scalar1=AC, scalar2=BC, op0=ALU.mult, op1=ALU.add),
                deps=[b1_, b2_, b3_, b4_, att_done[(pr, qb)]])
            att_done[(pr, qb)] = hn
    att_all = list(att_done.values())
    if DEBUG and b == 0:
        bar = P.barrier_handles()
        kd1 = P.dma("gpsimd", k.dbg_a[0], ATT, deps=bar)
        kd2 = P.dma("gpsimd", k.dbg_v[0], vplane, deps=bar)
        kd3 = P.dma("gpsimd", k.dbg_pv.bitcast(BF16)[:, 0:1024], k.SC[:, 0:1024], deps=bar)
        k.dbg_done += [kd1, kd2, kd3]
        dbg_fence(k, [kd1, kd2, kd3])

    MS = k.SMF[:, 0:512]
    gated_done = {}

    def evac_g(mc, tb, psum, deps):
        asl = ATT[:, mc * 2048 + tb * TB: mc * 2048 + (tb + 1) * TB]
        dsl = vplane[:, mc * 2048 + tb * TB: mc * 2048 + (tb + 1) * TB]
        e1 = P.emit("scalar", lambda sc, p=psum: sc.activation(out=MS, in_=p,
                                                               func=AF.Exp),
                    deps=deps + [k.smf_last])
        e2 = P.emit("scalar", lambda sc: sc.activation(out=MS, in_=MS, func=AF.Ln,
                                                       bias=k.COLS[0:128, 0:1]),
                    deps=[e1, k.h_ones])
        e3 = P.emit("scalar", lambda sc: sc.activation(out=MS, in_=MS, func=AF.Tanh),
                    deps=[e2])
        e4 = P.emit("vector", lambda v, p=psum: v.tensor_tensor(MS, p, MS, ALU.mult),
                    deps=[e3])
        e5 = P.emit("vector", lambda v, a=asl, d=dsl: v.tensor_tensor(d, MS, a, ALU.mult),
                    deps=[e4] + att_all)
        k.smf_last = e5
        gated_done[(mc, tb)] = e5
        return e5
    proj(k, lambda mc: k.dw["l1_wg_s"][mc], NCH, list(range(NCH)),
         lambda ci, tb: (hch(k, ci, tb), [h1done[(ci, tb)]]),
         evac_g, psb=(0, 1, 2, 3))

    if DEBUG and b == 0:
        bar = P.barrier_handles()
        kd4 = P.dma("gpsimd", k.dbg_h[0], vplane, deps=bar)
        k.dbg_done += [kd4]
        dbg_fence(k, [kd4])

    ln_h0_dual(k, gated_done)
    if DEBUG and b == 0:
        bar = P.barrier_handles()
        kd7 = P.dma("gpsimd", k.dbg_pv, k.Yb[:, 16384:18432].bitcast(F32)[:, 0:512]
                    if False else k.Yb.bitcast(F32)[:, 8192:8704], deps=bar)
        k.dbg_done += [kd7]
        dbg_fence(k, [kd7])

    def evac_wo1(mc, tb, psum, deps):
        dx = xch(k, mc, tb)
        ybsl = ATT[:, mc * 2048 + tb * TB: mc * 2048 + (tb + 1) * TB]
        h1 = P.emit("vector", lambda v, d=dx, p=psum: v.tensor_add(d, d, p),
                    deps=deps + [k.x_last[(mc, tb)]])
        k.x_last[(mc, tb)] = h1
        h2 = P.emit("vector", lambda v, d=ybsl, p=psum: v.tensor_add(d, d, p),
                    deps=deps + [k.yb_last[(mc, tb)]])
        k.yb_last[(mc, tb)] = h2
        if DEBUG and k.cur_batch == 0 and mc == 0 and tb == 0:
            kd9 = P.dma("gpsimd", k.dbg_pv, ATT.bitcast(F32)[:, 0:512], deps=[h2])
            k.dbg_done += [kd9]
            dbg_fence(k, [kd9])
        return h2
    proj(k, lambda mc: k.dw["l1_wo_s"][mc], NCH, list(range(NCH)),
         lambda ci, tb: (vplane[:, ci * 2048 + tb * TB: ci * 2048 + (tb + 1) * TB],
                         [gated_done[(ci, tb)]]),
         evac_wo1, psb=(0, 1, 2, 3))

    if DEBUG and b == 0:
        bar = P.barrier_handles()
        kd8 = P.dma("gpsimd", k.dbg_a[1], ATT, deps=bar)
        k.dbg_done += [kd8]
        dbg_fence(k, [kd8])

    z0 = ln_pass(k, None, lambda c, tb: (hch(k, c, tb), ()),
                 bct=k.SCf[:, 0:1536],
                 src_fn=lambda c, tb: (ATT[:, c * 2048 + tb * TB: c * 2048 + (tb + 1) * TB],
                                       [k.yb_last[(c, tb)]]))
    zdone = {}
    for c in range(NCH):
        h = P.emit("vector", lambda v, c=c: v.tensor_scalar(
            out=hch(k, c), in0=hch(k, c), scalar1=k.BIASC[:, 52 + c:53 + c],
            scalar2=k.BIASC[:, 44 + c:45 + c], op0=ALU.mult, op1=ALU.add),
            deps=[z0[(c, tb)] for tb in range(NTB)] + [h_l2w, h_l2b])
        for tb in range(NTB):
            zdone[(c, tb)] = h

    if DEBUG and b == 0:
        bar = P.barrier_handles()
        kd5 = P.dma("gpsimd", k.dbg_v[1], ATT, deps=bar)
        kd6 = P.dma("gpsimd", k.dbg_h[1], k.Hb, deps=bar)
        k.dbg_done += [kd5, kd6]
        dbg_fence(k, [kd5, kd6])

    for wave in range(4):
        tq_done = {}

        def evac_t(mc, tb, psum, deps, wave=wave):
            m = mc - 2 * wave
            dst = k.SC[:, m * 2048 + tb * TB: m * 2048 + (tb + 1) * TB]
            h = P.emit("scalar", lambda sc, dst=dst, p=psum, mc=mc: sc.activation(
                out=dst, in_=p, func=AF.Gelu, bias=k.BIASC[:, mc:mc + 1]),
                deps=deps + [h_b1])
            tq_done[(m, tb)] = h
            return h
        proj(k, lambda mc: k.dw["l1_w1_s"][mc], NCH, [2 * wave, 2 * wave + 1],
             lambda ci, tb: (hch(k, ci, tb), [zdone[(ci, tb)]]),
             evac_t, psb=(0, 1, 2, 3))

        def evac_x2(mc, tb, psum, deps, wave=wave):
            dst = xch(k, mc, tb)
            if wave == 0:
                h = P.emit("vector", lambda v, d=dst, p=psum, mc=mc:
                           v.scalar_tensor_tensor(out=d, in0=p,
                                                  scalar=k.BIASC[:, 8 + mc:9 + mc],
                                                  in1=d, op0=ALU.add, op1=ALU.add),
                           deps=deps + [h_b2, k.x_last[(mc, tb)]])
            else:
                h = P.emit("vector", lambda v, d=dst, p=psum: v.tensor_add(d, d, p),
                           deps=deps + [k.x_last[(mc, tb)]])
            k.x_last[(mc, tb)] = h
            return h
        proj(k, lambda mc, wave=wave: k.dw["l1_w2_s"][mc][:, wave * 256:(wave + 1) * 256], 2, list(range(NCH)),
             lambda ci, tb: (k.SC[:, ci * 2048 + tb * TB: ci * 2048 + (tb + 1) * TB],
                             [tq_done[(ci, tb)]]),
             evac_x2, psb=(0, 1, 2, 3))


def ln_h0_dual(k, gated_done):
    P = k.P
    ATT = k.Yb[:, 16384:32768]
    bct = k.SCf[:, 0:1536]
    bm, brs, Tt = bct[:, 0:512], bct[:, 512:1024], bct[:, 1024:1536]
    XBC, XSQ = k.SMB[:, 0:512], k.SMB[:, 512:1024]
    k.yb_last = {}
    guard = [k.last_tensor, k.smb_last]
    last_apply = list(g for g in guard if g is not None)
    for tb in range(NTB):
        h1 = h2 = None
        for c in range(NCH):
            xc = xch(k, c, tb)
            cast_h = P.emit("vector", lambda v, xc=xc: v.tensor_copy(out=XBC, in_=xc),
                            deps=[k.x_last[(c, tb)], h1])
            sq_h = P.emit("scalar", lambda s, xc=xc: s.activation(out=XSQ, in_=xc,
                                                                  func=AF.Square),
                          deps=[k.x_last[(c, tb)], h2])
            h1 = P.emit("tensor", lambda t, st=(c == 0), sp=(c == NCH - 1):
                        t.matmul(k.PB[6][0:1, :], k.COLB[:, 0:1], XBC, start=st, stop=sp),
                        deps=[cast_h, k.h_onesb, k.pb_last[6] if c == 0 else None])
            h2 = P.emit("tensor", lambda t, st=(c == 0), sp=(c == NCH - 1):
                        t.matmul(k.PB[7][0:1, :], k.COLB[:, 0:1], XSQ, start=st, stop=sp),
                        deps=[sq_h, k.h_onesb, k.pb_last[7] if c == 0 else None])
        rm = k.ROW[0:1, 0:512]
        rrs = k.ROW[0:1, 512:1024]
        rex2 = k.ROW[0:1, 1024:1536]
        rtmp = k.ROW[0:1, 1536:2048]
        la = last_apply[-1] if last_apply else None
        hm = P.emit("vector", lambda v: v.tensor_scalar_mul(rm, k.PB[6][0:1, :], 1.0 / D),
                    deps=[h1, la])
        k.pb_last[6] = hm
        hv = P.emit("vector", lambda v: v.tensor_scalar_mul(rex2, k.PB[7][0:1, :], 1.0 / D),
                    deps=[h2])
        k.pb_last[7] = hv
        hmsq = P.emit("vector", lambda v: v.tensor_mul(rtmp, rm, rm), deps=[hm])
        hvv = P.emit("vector", lambda v: v.tensor_sub(rex2, rex2, rtmp), deps=[hv, hmsq])
        epsc, epsh = k.constcol(1e-5)
        hsq = P.emit("scalar", lambda s, e=epsc: s.activation(out=rtmp, in_=rex2,
                                                              func=AF.Sqrt, bias=e[0:1]),
                     deps=[hvv, epsh])
        hrs = P.emit("vector", lambda v: v.reciprocal(rrs, rtmp), deps=[hsq, la])
        hbm = P.dma("scalar", bm, rm.unsqueeze(1).broadcast_to([1, 128, 512]),
                    deps=[hm] + last_apply)
        hbrs = P.dma("scalar", brs, rrs.unsqueeze(1).broadcast_to([1, 128, 512]),
                     deps=[hrs] + last_apply)
        last_apply = []
        for c in range(NCH):
            xc = xch(k, c, tb)
            ht = P.emit("vector", lambda v, xc=xc: v.tensor_sub(Tt, xc, bm),
                        deps=[hbm, k.x_last[(c, tb)]])
            ht2 = P.emit("vector", lambda v: v.tensor_mul(Tt, Tt, brs), deps=[ht, hbrs])
            hx = P.emit("vector", lambda v, xc=xc: v.tensor_add(xc, xc, Tt),
                        deps=[ht2, k.x_last[(c, tb)]])
            k.x_last[(c, tb)] = hx
            ybsl = ATT[:, c * 2048 + tb * TB: c * 2048 + (tb + 1) * TB]
            hy = P.emit("vector", lambda v, d=ybsl: v.tensor_copy(out=d, in_=Tt),
                        deps=[ht2, gated_done[(c, tb)], hx])
            k.yb_last[(c, tb)] = hy
            last_apply += [hx, hy]


def stage_l1_ffn(k, b):
    P = k.P
    front4 = P.barrier_handles()
    if DEBUG and b == 0:
        kde = P.dma("gpsimd", k.dbg_v[1].bitcast(F32)[:, 0:8192], k.X[:, 0:8192],
                    deps=front4)
        k.dbg_done += [kde]
        dbg_fence(k, [kde])
    hdone = ln_pass(k, None, lambda c, tb: (hch(k, c, tb), ()),
                    bct=k.SCf[:, 0:1536],
                    src_fn=lambda c, tb: (xch(k, c, tb), [k.x_last[(c, tb)]]))

    h_rw = P.dma("sync", k.WST[:, 0:128], k.dw["router_wT_s"][0],
                 deps=[k.wst_last[0]] + front4)
    h_rwb = P.emit("scalar", lambda e: e.activation(out=k.WB[:, 0:128],
                                                     in_=k.WST[:, 0:128],
                                                     func=AF.Identity),
                   deps=[h_rw, k.wb_last[0]])
    k.wst_last[0] = h_rwb
    SCORE = k.SCf[:, 1536:1792]
    sc_done = []
    for tt in range(16):
        mm = None
        for ci in range(NCH):
            hstat = hch(k, ci)[:, tt * 128:(tt + 1) * 128]
            mm = P.emit("tensor", lambda t, hs=hstat, w=k.WB[:, ci * 16:(ci + 1) * 16],
                        st=(ci == 0), sp=(ci == NCH - 1):
                        t.matmul(k.PB[6][:, 0:16], hs, w, start=st, stop=sp),
                        deps=[h_rwb, hdone[(ci, tt // 4)]] +
                             ([k.pb_last[6]] if ci == 0 else []))
        h = P.emit("scalar", lambda sc, tt=tt: sc.activation(
            out=SCORE[:, tt * 16:(tt + 1) * 16], in_=k.PB[6][:, 0:16], func=AF.Sigmoid),
            deps=[mm])
        k.pb_last[6] = h
        sc_done.append(h)
    k.wb_last[0] = mm
    k.last_tensor = mm

    SB = k.SCf[:, 1792:2048]
    P6 = k.SCf[:, 2048:2432]
    GS = k.SCf[:, 2432:2496]
    GR = k.SCf[:, 2496:2560]
    RANKE = k.SCf[:, 2560:2816]
    COMB = k.SCf[:, 2816:3072]
    CBC = k.Yb.bitcast(F32)[:, 0:256]
    _crow = k.Yb.bitcast(F32)[0:1, 2064:2320]
    h_cb0 = P.dma("scalar", _crow,
                  k.dw["corr_bias"][0:1, :].unsqueeze(1).broadcast_to([1, 16, 16]),
                  deps=front4)
    h_cb = P.dma("scalar", CBC, _crow.unsqueeze(1).broadcast_to([1, 128, 256]),
                 deps=[h_cb0])
    h_sb = P.emit("vector", lambda v: v.tensor_add(SB, SCORE, CBC), deps=sc_done + [h_cb])
    pairs = [(0, 1), (0, 2), (0, 3), (1, 2), (1, 3), (2, 3)]
    hps = []
    for pi, (i, j) in enumerate(pairs):
        src_i = SB.rearrange("p (t g e) -> p t g e", g=4, e=4)[:, :, :, i]
        src_j = SB.rearrange("p (t g e) -> p t g e", g=4, e=4)[:, :, :, j]
        dst = P6.rearrange("p (t g q) -> p t g q", g=4, q=6)[:, :, :, pi]
        hps.append(P.emit("vector", lambda v, a=src_i, b2=src_j, d=dst:
                          v.tensor_add(d, a, b2), deps=[h_sb]))
    h_gs = P.emit("vector", lambda v: v.tensor_reduce(
        GS, P6.rearrange("p (t g q) -> p t g q", g=4, q=6),
        axis=mybir.AxisListType.X, op=ALU.max), deps=hps)
    GLT = k.Yb.bitcast(F32)[:, 256:512]
    _grow = k.Yb.bitcast(F32)[0:1, 2320:2576]
    h_glt0 = P.dma("scalar", _grow,
                   k.dc["grp_lt"][0:1, :].unsqueeze(1).broadcast_to([1, 16, 16]),
                   deps=front4)
    h_glt = P.dma("scalar", GLT, _grow.unsqueeze(1).broadcast_to([1, 128, 256]),
                  deps=[h_glt0])
    GTMP = k.Yb.bitcast(F32)[:, 512:768]
    GTMP2 = k.Yb.bitcast(F32)[:, 768:1024]
    rep = GS.rearrange("p (t g) -> p t g", g=4).unsqueeze(3).broadcast_to([128, 16, 4, 4]).rearrange("p t g q -> p t q g")
    exp_ = GS.rearrange("p (t g) -> p t g", g=4).unsqueeze(3).broadcast_to([128, 16, 4, 4])
    h_g1 = P.emit("vector", lambda v: v.tensor_tensor(GTMP, rep, exp_, ALU.is_gt),
                  deps=[h_gs])
    h_g2 = P.emit("vector", lambda v: v.tensor_tensor(GTMP2, rep, exp_, ALU.is_equal),
                  deps=[h_gs])
    h_g3 = P.emit("vector", lambda v: v.tensor_mul(GTMP2, GTMP2, GLT), deps=[h_g2, h_glt])
    h_g4 = P.emit("vector", lambda v: v.tensor_add(GTMP, GTMP, GTMP2), deps=[h_g1, h_g3])
    h_g5 = P.emit("vector", lambda v: v.tensor_reduce(
        GR, GTMP.rearrange("p (t g q) -> p t g q", g=4, q=4),
        axis=mybir.AxisListType.X, op=ALU.add), deps=[h_g4])
    h_gm = P.emit("vector", lambda v: v.tensor_scalar(out=GR, in0=GR, scalar1=2.0,
                                                      scalar2=None, op0=ALU.is_lt),
                  deps=[h_g5])
    GMU = k.Yb.bitcast(F32)[:, 1024:1280].bitcast(mybir.dt.uint8)[:, 0:256]
    gme = GR.rearrange("p (t g) -> p t g", g=4).unsqueeze(3).broadcast_to([128, 16, 4, 4])
    h_gm8 = P.emit("vector", lambda v: v.tensor_copy(out=GMU, in_=gme), deps=[h_gm])
    NEGB = k.Yb.bitcast(F32)[:, 1280:1536]
    h_nb = P.emit("vector", lambda v: v.memset(NEGB, -1e9), deps=front4)
    SBM = k.Yb.bitcast(F32)[:, 1536:1792]
    h_sbm = P.emit("vector", lambda v: v.select(out=SBM, mask=GMU, on_true=SB,
                                                on_false=NEGB), deps=[h_gm8, h_sb, h_nb])
    SBM16 = k.SC[:, 1024:1280]
    h_s16 = P.emit("vector", lambda v: v.tensor_copy(out=SBM16, in_=SBM), deps=[h_sbm])
    ELT = k.SC[:, 1280:1536]
    h_elt = P.dma("scalar", ELT,
                  k.dc["moe_lt"][0:1, :].unsqueeze(1).broadcast_to([1, 128, 256]),
                  deps=sc_done)
    GTB = k.SC[:, 1536:2560]
    EQB = k.SC[:, 0:1024]
    rank_parts = []
    for rnd in range(4):
        tsl = slice(rnd * 4, rnd * 4 + 4)
        base = SBM16.rearrange("p (t e) -> p t e", e=16)[:, tsl, :]
        repe = base.unsqueeze(3).broadcast_to([128, 4, 16, 16]).rearrange("p t e q -> p t q e")
        expe = base.unsqueeze(3).broadcast_to([128, 4, 16, 16])
        eltv = ELT.rearrange("p (t x) -> p t x", x=16).unsqueeze(1).broadcast_to([128, 4, 16, 16]).rearrange("p q t x -> p q (t x)") if False else None
        eltv = ELT.rearrange("p (e q) -> p e q", q=16).unsqueeze(1).broadcast_to([128, 4, 16, 16])
        h1 = P.emit("vector", lambda v, a=repe, b2=expe: v.tensor_tensor(
            GTB, a, b2, ALU.is_gt), deps=[h_s16] + rank_parts[-1:])
        h2 = P.emit("vector", lambda v, a=repe, b2=expe: v.tensor_tensor(
            EQB, a, b2, ALU.is_equal), deps=[h_s16] + rank_parts[-1:])
        h3 = P.emit("vector", lambda v, e=eltv: v.tensor_tensor(EQB, EQB, e, ALU.mult),
                    deps=[h2, h_elt])
        h4 = P.emit("vector", lambda v: v.tensor_add(GTB, GTB, EQB), deps=[h1, h3])
        h5 = P.emit("vector", lambda v, rnd=rnd: v.tensor_reduce(
            RANKE[:, rnd * 64:(rnd + 1) * 64],
            GTB.rearrange("p (t e q) -> p t e q", e=16, q=16),
            axis=mybir.AxisListType.X, op=ALU.add), deps=[h4])
        rank_parts.append(h5)
    SEL = k.Yb.bitcast(F32)[:, 1792:2048]
    h_sel = P.emit("vector", lambda v: v.tensor_scalar(out=SEL, in0=RANKE, scalar1=4.0,
                                                       scalar2=None, op0=ALU.is_lt),
                   deps=rank_parts)
    h_c1 = P.emit("vector", lambda v: v.tensor_mul(COMB, SCORE, SEL), deps=[h_sel] + sc_done)
    RS = k.Yb.bitcast(F32)[:, 2048:2064]
    h_c2 = P.emit("vector", lambda v: v.tensor_reduce(
        RS, COMB.rearrange("p (t e) -> p t e", e=16), axis=mybir.AxisListType.X,
        op=ALU.add), deps=[h_c1])
    h_c3 = P.emit("vector", lambda v: v.tensor_scalar_add(RS, RS, 1e-20), deps=[h_c2])
    h_c4 = P.emit("vector", lambda v: v.reciprocal(RS, RS), deps=[h_c3])
    rsb = RS.rearrange("p t -> p t").unsqueeze(2).broadcast_to([128, 16, 16])
    h_c5 = P.emit("vector", lambda v, r=rsb: v.tensor_tensor(COMB, COMB, r, ALU.mult),
                  deps=[h_c4])
    k.h_comb = h_c5
    if DEBUG and b == 0:
        kda = P.dma("gpsimd", k.dbg_pv, k.SCf[:, 1536:2048], deps=[h_c5] + sc_done)
        kdb = P.dma("gpsimd", k.dbg_p.bitcast(F32)[:, 0:1024],
                    k.SCf[:, 2048:3072], deps=[h_c5] + rank_parts)
        k.dbg_done += [kda, kdb]
        dbg_fence(k, [kda, kdb])

    COMBC = k.SCf[:, 2816:3072]
    hs_plane = k.Yb
    MS = k.SMF[:, 0:512]
    hs_done = {}
    half = 0
    for m in range(16):
        wdst = k.WST[:, half * 1024:(half + 1) * 1024]
        hl = P.dma("sync", wdst,
                   k.dw["sh_guT_s"][m],
                   deps=[k.wst_last[half]])
        wb = k.WB[:, half * 1024:(half + 1) * 1024]
        hc = P.emit("scalar", lambda e, a=wb, s=wdst: e.activation(
            out=a, in_=s, func=AF.Identity), deps=[hl, k.wb_last[half]])
        k.wst_last[half] = hc
        wdst2 = k.WST[:, (1 - half) * 1024:(2 - half) * 1024]
        hl2 = P.dma("sync", wdst2,
                    k.dw["sh_guT_s"][16 + m],
                    deps=[k.wst_last[1 - half]])
        wb2 = k.WB[:, (1 - half) * 1024:(2 - half) * 1024]
        hc2 = P.emit("scalar", lambda e, a=wb2, s=wdst2: e.activation(
            out=a, in_=s, func=AF.Identity), deps=[hl2, k.wb_last[1 - half]])
        k.wst_last[1 - half] = hc2
        mmg = mmu = None
        for ci in range(NCH):
            for tb in range(NTB):
                mmg = P.emit("tensor", lambda t, w=wb[:, ci * 128:(ci + 1) * 128],
                             ci=ci, tb=tb, st=(ci == 0), sp=(ci == NCH - 1):
                             t.matmul(k.PB[tb], w, hch(k, ci, tb), start=st, stop=sp),
                             deps=[hc, hdone[(ci, tb)]] +
                                  ([k.pb_last[tb]] if ci == 0 else []))
                mmu = P.emit("tensor", lambda t, w=wb2[:, ci * 128:(ci + 1) * 128],
                             ci=ci, tb=tb, st=(ci == 0), sp=(ci == NCH - 1):
                             t.matmul(k.PB[4 + tb], w, hch(k, ci, tb), start=st, stop=sp),
                             deps=[hc2, hdone[(ci, tb)]] +
                                  ([k.pb_last[4 + tb]] if ci == 0 else []))
        k.wb_last[half] = mmg
        k.wb_last[1 - half] = mmu
        k.last_tensor = mmu
        for tb in range(NTB):
            e1 = P.emit("scalar", lambda sc, p=k.PB[tb]: sc.activation(
                out=MS, in_=p, func=AF.Silu), deps=[mmg, k.smf_last])
            dst = hs_plane[:, m * 2048 + tb * TB: m * 2048 + (tb + 1) * TB]
            e2 = P.emit("vector", lambda v, d=dst, p=k.PB[4 + tb]: v.tensor_tensor(
                d, MS, p, ALU.mult), deps=[e1, h_c5, mmu])
            k.smf_last = e2
            k.pb_last[tb] = e2
            k.pb_last[4 + tb] = e2
            hs_done[(m, tb)] = e2

    def evac_xs(mc, tb, psum, deps):
        dst = xch(k, mc, tb)
        h = P.emit("vector", lambda v, d=dst, p=psum: v.tensor_add(d, d, p),
                   deps=deps + [k.x_last[(mc, tb)]])
        k.x_last[(mc, tb)] = h
        return h
    proj(k, lambda mc: k.dw["sh_downT_s"][mc], 16, list(range(NCH)),
         lambda ci, tb: (hs_plane[:, ci * 2048 + tb * TB: ci * 2048 + (tb + 1) * TB],
                         [hs_done[(ci, tb)]]),
         evac_xs, psb=(0, 1, 2, 3))
    k.h_bcc_last = k.last_tensor
    if DEBUG and b == 0:
        bar = P.barrier_handles()
        kdc = P.dma("gpsimd", k.dbg_h[0], hs_plane[:, 0:16384], deps=bar)
        kdc2 = P.dma("gpsimd", k.dbg_v[0], hs_plane[:, 16384:32768], deps=bar)
        kdd = P.dma("gpsimd", k.dbg_a[0].bitcast(F32)[:, 0:8192], k.X[:, 0:8192], deps=bar)
        kdd2 = P.dma("gpsimd", k.dbg_a[1].bitcast(F32)[:, 0:8192], k.X[:, 8192:16384], deps=bar)
        k.dbg_done += [kdc, kdc2, kdd, kdd2]
        dbg_fence(k, [kdc, kdc2, kdd, kdd2])

    for e in range(16):
        TMPC = k.ROUT[:, 16:32]
        hcp = P.emit("vector", lambda v, e=e: v.tensor_copy(
            out=TMPC, in_=COMB[:, e::16]),
            deps=[k.h_comb, getattr(k, "h_tmpc_last", None)])
        crow = k.Yb.bitcast(F32)[0:1, 2064:4112]
        hcr = P.dma("scalar", crow, TMPC,
                    deps=[hcp, getattr(k, "h_bcc_last", None)])
        k.h_tmpc_last = hcr
        BCC = k.Yb.bitcast(F32)[:, 4112:6160]
        hbc = P.dma("scalar", BCC, crow.unsqueeze(1).broadcast_to([1, 128, 2048]),
                    deps=[hcr])
        he_done = {}
        for m in range(2):
            wdst = k.WST[:, 0:1024]
            hl = P.dma("sync", wdst,
                       k.dw["moe_gu_s"][e][m],
                       deps=[k.wst_last[0]])
            wb = k.WB[:, 0:1024]
            hc = P.emit("scalar", lambda e, a=wb, s=wdst: e.activation(
                out=a, in_=s, func=AF.Identity), deps=[hl, k.wb_last[0]])
            k.wst_last[0] = hc
            wdst2 = k.WST[:, 1024:2048]
            hl2 = P.dma("sync", wdst2,
                        k.dw["moe_gu_s"][e][2 + m],
                        deps=[k.wst_last[1]])
            wb2 = k.WB[:, 1024:2048]
            hc2 = P.emit("scalar", lambda e, a=wb2, s=wdst2: e.activation(
                out=a, in_=s, func=AF.Identity), deps=[hl2, k.wb_last[1]])
            k.wst_last[1] = hc2
            mmg = mmu = None
            for ci in range(NCH):
                for tb in range(NTB):
                    mmg = P.emit("tensor", lambda t, w=wb[:, ci * 128:(ci + 1) * 128],
                                 ci=ci, tb=tb, st=(ci == 0), sp=(ci == NCH - 1):
                                 t.matmul(k.PB[tb], w, hch(k, ci, tb), start=st, stop=sp),
                                 deps=[hc, hdone[(ci, tb)]] +
                                      ([k.pb_last[tb]] if ci == 0 else []))
                    mmu = P.emit("tensor", lambda t, w=wb2[:, ci * 128:(ci + 1) * 128],
                                 ci=ci, tb=tb, st=(ci == 0), sp=(ci == NCH - 1):
                                 t.matmul(k.PB[4 + tb], w, hch(k, ci, tb), start=st, stop=sp),
                                 deps=[hc2, hdone[(ci, tb)]] +
                                      ([k.pb_last[4 + tb]] if ci == 0 else []))
            k.wb_last[0] = mmg
            k.wb_last[1] = mmu
            k.last_tensor = mmu
            for tb in range(NTB):
                e1 = P.emit("scalar", lambda sc, p=k.PB[tb]: sc.activation(
                    out=MS, in_=p, func=AF.Silu), deps=[mmg, k.smf_last])
                e2 = P.emit("vector", lambda v, p=k.PB[4 + tb]: v.tensor_tensor(
                    MS, MS, p, ALU.mult), deps=[e1, mmu])
                dst = k.Yb[:, 0:4096][:, m * 2048 + tb * TB: m * 2048 + (tb + 1) * TB]
                bccv = BCC.rearrange("p (a b) -> p b a", b=16)[:, 4 * tb:4 * tb + 4, :]
                e3 = P.emit("vector", lambda v, d=dst, bv=bccv: v.tensor_tensor(
                    d, MS, bv, ALU.mult), deps=[e2, hbc])
                k.smf_last = e3
                k.pb_last[tb] = e3
                k.pb_last[4 + tb] = e3
                he_done[(m, tb)] = e3
        k.h_bcc_last = he_done[(1, NTB - 1)]

        def evac_xe(mc, tb, psum, deps):
            dst = xch(k, mc, tb)
            h = P.emit("vector", lambda v, d=dst, p=psum: v.tensor_add(d, d, p),
                       deps=deps + [k.x_last[(mc, tb)]])
            k.x_last[(mc, tb)] = h
            return h
        proj(k, lambda mc, e=e: k.dw["moe_down_s"][e][mc], 2, list(range(NCH)),
             lambda ci, tb: (k.Yb[:, ci * 2048 + tb * TB: ci * 2048 + (tb + 1) * TB],
                             [he_done[(ci, tb)]]),
             evac_xe, psb=(0, 1, 2, 3))
